# version 08
# baseline (speedup 1.0000x reference)
import numpy as np
from contextlib import ExitStack

import concourse.bass as bass
import concourse.mybir as mybir
from concourse.bass_utils import run_bass_kernel_spmd

F32 = mybir.dt.float32
BF16 = mybir.dt.bfloat16
I32 = mybir.dt.int32
AF = mybir.ActivationFunctionType
ALU = mybir.AluOpType
AX = mybir.AxisListType

D = 2048
S_FULL = 4096
S_OWN = 2048
NH = 16
DH = 128
FI = 8192
NCI = D // 128
SCALE = DH ** -0.5
ZP = 128.0

DBG = False


def build_program(dbg=False, stop_after=None):
    phases = ["A", "B1", "W", "B2", "B3", "C"]
    if stop_after is not None:
        phases = phases[: phases.index(stop_after) + 1]

    nc = bass.Bass("TRN2", target_bir_lowering=False)

    x = nc.dram_tensor("x", [S_FULL, D], F32, kind="ExternalInput").ap()
    wqr = nc.dram_tensor("wqr", [NH, 128, D], F32, kind="ExternalInput").ap()
    wkr = nc.dram_tensor("wkr", [NH, 128, D], F32, kind="ExternalInput").ap()
    wvr = nc.dram_tensor("wvr", [4, 128, FI], F32, kind="ExternalInput").ap()
    wor = nc.dram_tensor("wor", [128, NH, D], F32, kind="ExternalInput").ap()
    gwT = nc.dram_tensor("gwT", [D, FI], I32, kind="ExternalInput").ap()
    uwT = nc.dram_tensor("uwT", [D, FI], I32, kind="ExternalInput").ap()
    dwT = nc.dram_tensor("dwT", [FI, D], I32, kind="ExternalInput").ap()
    gsc = nc.dram_tensor("gsc", [64, 128], F32, kind="ExternalInput").ap()
    usc = nc.dram_tensor("usc", [64, 128], F32, kind="ExternalInput").ap()
    dscr = nc.dram_tensor("dscr", [1, D], F32, kind="ExternalInput").ap()
    eye = nc.dram_tensor("eye", [128, 128], F32, kind="ExternalInput").ap()
    out = nc.dram_tensor("out", [S_OWN, D], F32, kind="ExternalOutput").ap()

    sk = "ExternalOutput" if dbg else "Internal"
    xnof = nc.dram_tensor("xnof", [4, 128, NCI, 512], BF16, kind=sk).ap()
    kfm_h = nc.dram_tensor("kfm_h", [NH * 128, S_OWN], BF16, kind=sk).ap()
    kfm_f = nc.dram_tensor("kfm_f", [2 * NH * 128, S_OWN], BF16,
                           kind="Internal").ap()
    qfm = nc.dram_tensor("qfm", [NH, DH, S_OWN], BF16, kind=sk).ap()
    vtm_hl = [nc.dram_tensor(f"vtm_h{dc}", [S_OWN, 512], BF16, kind=sk).ap()
              for dc in range(4)]
    vtm_fl = [nc.dram_tensor(f"vtm_f{dc}", [2 * S_OWN, 512], BF16,
                             kind="Internal").ap() for dc in range(4)]
    bnd_d = nc.dram_tensor("bnd_d", [2, NH], F32, kind="Internal").ap()
    bnd_g = nc.dram_tensor("bnd_g", [2, NH], F32, kind="Internal").ap()
    stm = nc.dram_tensor("stm", [S_OWN, D], BF16, kind=sk).ap()
    gwb_d = nc.dram_tensor("gwb_d", [16, 128, NCI, 512], BF16, kind="Internal").ap()
    uwb_d = nc.dram_tensor("uwb_d", [16, 128, NCI, 512], BF16, kind="Internal").ap()
    dwb_d = nc.dram_tensor("dwb_d", [4, 4, 128, 16, 512], BF16,
                           kind=("ExternalOutput" if dbg else "Internal")).ap()
    atdbg = None
    if dbg:
        atdbg = nc.dram_tensor("atdbg", [128, NH * S_OWN], BF16,
                               kind="ExternalOutput").ap()
        htdbg = nc.dram_tensor("htdbg", [128, 64 * 512], BF16,
                               kind="ExternalOutput").ap()
        rowdbg = nc.dram_tensor("rowdbg", [2, 512], BF16,
                                kind="ExternalOutput").ap()
        l0dbg = nc.dram_tensor("l0dbg", [4, 512], F32,
                               kind="ExternalOutput").ap()
        bc0dbg = nc.dram_tensor("bc0dbg", [128, 1], F32,
                                kind="ExternalOutput").ap()

    es = ExitStack()

    def sem(name):
        return es.enter_context(nc.semaphore(name))

    with es:
        vss = es.enter_context(nc.semaphore("vss")); vssn = [0]
        sss = es.enter_context(nc.semaphore("sss")); sssn = [0]
        pss = es.enter_context(nc.semaphore("pss")); pssn = [0]

        def vmark(instr):
            vssn[0] += 1
            instr.then_inc(vss, 1)

        def vwait(e):
            if vssn[0]:
                e.wait_ge(vss, vssn[0])

        def smark(instr):
            sssn[0] += 1
            instr.then_inc(sss, 1)

        def swait(e):
            if sssn[0]:
                e.wait_ge(sss, sssn[0])

        def pmark(instr):
            pssn[0] += 1
            instr.then_inc(pss, 1)

        def pwait(e):
            if pssn[0]:
                e.wait_ge(pss, pssn[0])

        ones_col = es.enter_context(nc.sbuf_tensor("ones_col", [128, 1], BF16))
        ones_row1 = es.enter_context(nc.sbuf_tensor("ones_row1", [1, 128], BF16))
        ones_row512 = es.enter_context(
            nc.sbuf_tensor("ones_row512", [1, 512], BF16))
        epscol = es.enter_context(nc.sbuf_tensor("epscol", [128, 1], F32))
        mq2 = es.enter_context(nc.sbuf_tensor("mq2", [1, NH], F32))
        mk2 = es.enter_context(nc.sbuf_tensor("mk2", [1, NH], F32))
        bnd_q = es.enter_context(nc.sbuf_tensor("bnd_q", [1, NH], F32))
        bnd_k = es.enter_context(nc.sbuf_tensor("bnd_k", [1, NH], F32))
        gscol = es.enter_context(nc.sbuf_tensor("gscol", [128, 64], F32))
        uscol = es.enter_context(nc.sbuf_tensor("uscol", [128, 64], F32))
        ntmp = es.enter_context(nc.sbuf_tensor("ntmp", [1, 2], F32))

        p0_ld = sem("p0_ld")
        with nc.Block("P0") as blk:
            @blk.sync
            def _(e):
                n = 0
                for j in range(64):
                    e.dma_start(out=gscol[:, j:j + 1],
                                in_=gsc[j:j + 1, :].rearrange("o p -> p o")
                                ).then_inc(p0_ld, 16); n += 1
                    e.dma_start(out=uscol[:, j:j + 1],
                                in_=usc[j:j + 1, :].rearrange("o p -> p o")
                                ).then_inc(p0_ld, 16); n += 1
                e.wait_ge(p0_ld, 16 * n)

            @blk.vector
            def _(e):
                e.memset(ones_col[:], 1.0)
                e.memset(ones_row1[:], 1.0)
                e.memset(ones_row512[:], 1.0)
                e.memset(epscol[:], 1e-5)
                e.memset(mq2[:], 0.0)
                e.memset(mk2[:], 0.0)

        ab = ExitStack()
        xn_fm = ab.enter_context(nc.sbuf_tensor("xn_fm", [128, NCI * S_OWN],
                                                BF16))
        pp = [ab.enter_context(nc.psum_tensor(f"pp{i}", [128, 512], F32))
              for i in range(2)]
        pmisc = ab.enter_context(nc.psum_tensor("pmisc", [128, 512], F32))

        aa = ExitStack()
        seye = aa.enter_context(nc.sbuf_tensor("seye", [128, 128], F32))
        eye_bf = aa.enter_context(nc.sbuf_tensor("eye_bf", [128, 128], BF16))
        xbuf = [aa.enter_context(nc.sbuf_tensor(f"xbuf{i}", [128, D], F32))
                for i in range(2)]
        xn_tm = [aa.enter_context(nc.sbuf_tensor(f"xn_tm{i}", [128, D], BF16))
                 for i in range(2)]
        colsA = [aa.enter_context(nc.sbuf_tensor(f"colsA{i}", [128, 8], F32))
                 for i in range(2)]
        sqscr = aa.enter_context(nc.sbuf_tensor("sqscr", [128, D], F32))
        ptp = [aa.enter_context(nc.psum_tensor(f"ptp{i}", [128, 128], BF16))
               for i in range(2)]
        a_eld = sem("a_eld")
        a_ld = [sem("a_ld0"), sem("a_ld1")]
        a_sq = sem("a_sq"); a_st = sem("a_st")
        a_rs = sem("a_rs"); a_nb = sem("a_nb"); a_xn = sem("a_xn")
        a_tp = sem("a_tp"); a_ev = sem("a_ev"); a_wb = sem("a_wb")
        NT_A = S_OWN // 128

        with nc.Block("A") as blk:
            @blk.sync
            def _(e):
                e.dma_start(out=seye[:], in_=eye).then_inc(a_eld, 16)
                for i in range(NT_A):
                    if i >= 2:
                        e.wait_ge(a_xn, i - 1)
                    e.dma_start(out=xbuf[i % 2][:],
                                in_=x[i * 128:(i + 1) * 128, :]
                                ).then_inc(a_ld[i % 2], 16)

            @blk.scalar
            def _(e):
                for i in range(NT_A):
                    e.wait_ge(a_ld[i % 2], 16 * (i // 2 + 1))
                    if i >= 2:
                        e.wait_ge(a_st, i - 1)
                    e.activation(out=sqscr[:], in_=xbuf[i % 2][:], func=AF.Square,
                                 accum_out=colsA[i % 2][:, 2:3]).then_inc(a_sq, 1)
                    e.wait_ge(a_st, i + 1)
                    smark(e.activation(out=colsA[i % 2][:, 6:7],
                                       in_=colsA[i % 2][:, 5:6],
                                       func=AF.Ln, bias=epscol[:, 0:1]))
                    swait(e)
                    e.activation(out=colsA[i % 2][:, 7:8], in_=colsA[i % 2][:, 6:7],
                                 func=AF.Exp, scale=-0.5).then_inc(a_rs, 1)

            @blk.vector
            def _(e):
                e.wait_ge(a_eld, 16)
                e.tensor_copy(out=eye_bf[:], in_=seye[:])
                for i in range(NT_A):
                    c = colsA[i % 2]
                    e.wait_ge(a_ld[i % 2], 16 * (i // 2 + 1))
                    if i >= 2:
                        e.wait_ge(a_xn, i - 1)
                    vmark(e.reduce_sum(out=c[:, 0:1], in_=xbuf[i % 2][:],
                                       axis=AX.X, negate=True))
                    vwait(e)
                    vmark(e.tensor_scalar(out=c[:, 1:2], in0=c[:, 0:1],
                                          scalar1=1.0 / D, scalar2=None,
                                          op0=ALU.mult))
                    e.wait_ge(a_sq, i + 1)
                    vmark(e.tensor_scalar(out=c[:, 3:4], in0=c[:, 2:3],
                                          scalar1=1.0 / D, scalar2=None,
                                          op0=ALU.mult))
                    vwait(e)
                    vmark(e.tensor_tensor(out=c[:, 4:5], in0=c[:, 1:2],
                                          in1=c[:, 1:2], op=ALU.mult))
                    vwait(e)
                    e.tensor_tensor(out=c[:, 5:6], in0=c[:, 3:4], in1=c[:, 4:5],
                                    op=ALU.subtract).then_inc(a_st, 1)
                    e.wait_ge(a_rs, i + 1)
                    vmark(e.tensor_scalar(out=c[:, 4:5], in0=c[:, 1:2],
                                          scalar1=c[:, 7:8], scalar2=None,
                                          op0=ALU.mult))
                    if i >= 2:
                        e.wait_ge(a_tp, 16 * (i - 1))
                    vwait(e)
                    e.tensor_scalar(out=xn_tm[i % 2][:], in0=xbuf[i % 2][:],
                                    scalar1=c[:, 7:8], scalar2=c[:, 4:5],
                                    op0=ALU.mult, op1=ALU.add).then_inc(a_xn, 1)
                    for ci in range(NCI):
                        g = i * NCI + ci
                        e.wait_ge(a_tp, g + 1)
                        e.tensor_copy(
                            out=xn_fm[:, ci * S_OWN + i * 128:
                                      ci * S_OWN + (i + 1) * 128],
                            in_=ptp[g % 2][:]).then_inc(a_ev, 1)

            @blk.tensor
            def _(e):
                for i in range(NT_A):
                    e.wait_ge(a_xn, i + 1)
                    for ci in range(NCI):
                        g = i * NCI + ci
                        if g >= 2:
                            e.wait_ge(a_ev, g - 1)
                        e.transpose(ptp[g % 2][:],
                                    xn_tm[i % 2][:, ci * 128:(ci + 1) * 128],
                                    eye_bf[:]).then_inc(a_tp, 1)

            @blk.gpsimd
            def _(e):
                e.wait_ge(a_ev, NT_A * NCI)
                for tc in range(4):
                    for ci in range(NCI):
                        e.dma_start(
                            out=xnof[tc, :, ci, :],
                            in_=xn_fm[:, ci * S_OWN + tc * 512:
                                      ci * S_OWN + tc * 512 + 512]
                            ).then_inc(a_wb, 16)
                e.wait_ge(a_wb, 16 * 64)
        aa.close()

        if "B1" in phases:
            b1a = ExitStack()
            wqk_f32 = [b1a.enter_context(
                nc.sbuf_tensor(f"wqkf{i}", [128, D], F32)) for i in range(2)]
            wq_bf = [b1a.enter_context(
                nc.sbuf_tensor(f"wqbf{i}", [128, D], BF16)) for i in range(2)]
            wk_bf = [b1a.enter_context(
                nc.sbuf_tensor(f"wkbf{i}", [128, D], BF16)) for i in range(2)]
            st3 = [b1a.enter_context(
                nc.sbuf_tensor(f"st3_{i}", [128, 512], BF16)) for i in range(3)]
            nsq = [b1a.enter_context(
                nc.sbuf_tensor(f"nsq{i}", [128, 512], BF16)) for i in range(2)]

            b1_lwq = sem("b1_lwq"); b1_lwk = sem("b1_lwk")
            b1_cwq = sem("b1_cwq"); b1_cwk = sem("b1_cwk")
            b1_mm = sem("b1_mm"); b1_ev = sem("b1_ev")
            b1_wr = [sem("b1_wr0"), sem("b1_wr1"), sem("b1_wr2")]
            b1_sq = sem("b1_sq"); b1_ns = sem("b1_ns"); b1_nm = sem("b1_nm")
            NG = 8

            with nc.Block("B1a") as blk:
                @blk.sync
                def _(e):
                    for h in range(NH):
                        if h >= 1:
                            e.wait_ge(b1_cwq, h)
                        e.dma_start(out=wqk_f32[0][:], in_=wqr[h]
                                    ).then_inc(b1_lwq, 16)
                        if h >= 1:
                            e.wait_ge(b1_cwk, h)
                        e.dma_start(out=wqk_f32[1][:], in_=wkr[h]
                                    ).then_inc(b1_lwk, 16)

                @blk.vector
                def _(e):
                    for h in range(NH):
                        e.wait_ge(b1_lwq, 16 * (h + 1))
                        if h >= 2:
                            e.wait_ge(b1_mm, NG * (h - 1))
                        e.tensor_copy(out=wq_bf[h % 2][:], in_=wqk_f32[0][:]
                                      ).then_inc(b1_cwq, 1)
                        e.wait_ge(b1_lwk, 16 * (h + 1))
                        e.tensor_copy(out=wk_bf[h % 2][:], in_=wqk_f32[1][:]
                                      ).then_inc(b1_cwk, 1)
                        for j in range(NG):
                            g = h * NG + j
                            e.wait_ge(b1_ev, g + 1)
                            if g >= 2:
                                e.wait_ge(b1_ns, g - 1)
                            e.tensor_tensor(out=nsq[g % 2][:], in0=st3[g % 3][:],
                                            in1=st3[g % 3][:], op=ALU.mult
                                            ).then_inc(b1_sq, 1)
                            e.wait_ge(b1_ns, g + 1)
                            dst = mq2 if j < 4 else mk2
                            vmark(e.reduce_max(out=ntmp[0:1, 0:1],
                                               in_=pmisc[0:1, 0:512], axis=AX.X))
                            vwait(e)
                            e.tensor_tensor(out=dst[0:1, h:h + 1],
                                            in0=dst[0:1, h:h + 1],
                                            in1=ntmp[0:1, 0:1], op=ALU.max
                                            ).then_inc(b1_nm, 1)

                @blk.tensor
                def _(e):
                    def colsum(gp):
                        e.wait_ge(b1_sq, gp + 1)
                        if gp >= 1:
                            e.wait_ge(b1_nm, gp)
                        e.matmul(pmisc[0:1, 0:512], ones_col[:, 0:1],
                                 nsq[gp % 2][:], start=True, stop=True
                                 ).then_inc(b1_ns, 1)

                    for h in range(NH):
                        for j in range(NG):
                            g = h * NG + j
                            isq = j < 4
                            wbf = wq_bf[h % 2] if isq else wk_bf[h % 2]
                            tc = j if isq else j - 4
                            if isq and j == 0:
                                e.wait_ge(b1_cwq, h + 1)
                            if (not isq) and j == 4:
                                e.wait_ge(b1_cwk, h + 1)
                            if g >= 2:
                                e.wait_ge(b1_ev, g - 1)
                            for ci in range(NCI):
                                mm = e.matmul(
                                    pp[g % 2][:],
                                    wbf[:, ci * 128:(ci + 1) * 128],
                                    xn_fm[:, ci * S_OWN + tc * 512:
                                          ci * S_OWN + tc * 512 + 512],
                                    start=(ci == 0), stop=(ci == NCI - 1))
                                if ci == NCI - 1:
                                    mm.then_inc(b1_mm, 1)
                            if j >= 1:
                                colsum(g - 1)
                            if j == NG - 1:
                                colsum(g)

                @blk.scalar
                def _(e):
                    for h in range(NH):
                        for j in range(NG):
                            g = h * NG + j
                            isq = j < 4
                            tc = j if isq else j - 4
                            e.wait_ge(b1_mm, g + 1)
                            if g >= 3:
                                e.wait_ge(b1_wr[g % 3], 16 * (g // 3))
                                e.wait_ge(b1_sq, g - 2)
                            e.activation(out=st3[g % 3][:], in_=pp[g % 2][:],
                                         func=AF.Copy,
                                         scale=(SCALE if isq else 1.0)
                                         ).then_inc(b1_ev, 1)
                            e.wait_ge(b1_ev, g + 1)
                            if isq:
                                dst = qfm[h, :, tc * 512:tc * 512 + 512]
                            else:
                                dst = kfm_h[h * 128:(h + 1) * 128,
                                            tc * 512:tc * 512 + 512]
                            e.dma_start(out=dst, in_=st3[g % 3][:]
                                        ).then_inc(b1_wr[g % 3], 16)
                    for p in range(3):
                        tot = sum(1 for g in range(NH * NG) if g % 3 == p)
                        e.wait_ge(b1_wr[p], 16 * tot)
            b1a.close()

            b1b = ExitStack()
            wv_f32 = b1b.enter_context(nc.sbuf_tensor("wvf", [128, 4096], F32))
            wv_bf = [b1b.enter_context(
                nc.sbuf_tensor(f"wvbf{i}", [128, FI], BF16)) for i in range(2)]
            st3b = [b1b.enter_context(
                nc.sbuf_tensor(f"st3b{i}", [128, 512], BF16)) for i in range(3)]
            bv_lw = sem("bv_lw"); bv_cw = sem("bv_cw"); bv_mm = sem("bv_mm")
            bv_ev = sem("bv_ev")
            bv_wr = [sem("bv_wr0"), sem("bv_wr1"), sem("bv_wr2")]
            cc_k = sem("cc_k"); cc_b = sem("cc_b"); cc_t = sem("cc_t")
            cc_vd = [sem(f"cc_vd{dc}") for dc in range(4)]
            RG_PAIR = [[2 * i, 2 * i + 1] for i in range(4)]

            with nc.Block("B1b") as blk:
                @blk.sync
                def _(e):
                    for dc in range(4):
                        for hf in range(2):
                            u = dc * 2 + hf
                            if u >= 1:
                                e.wait_ge(bv_cw, u)
                            e.dma_start(out=wv_f32[:],
                                        in_=wvr[dc, :, hf * 4096:(hf + 1) * 4096]
                                        ).then_inc(bv_lw, 16)

                @blk.vector
                def _(e):
                    for dc in range(4):
                        for hf in range(2):
                            u = dc * 2 + hf
                            e.wait_ge(bv_lw, 16 * (u + 1))
                            if dc >= 2 and hf == 0:
                                e.wait_ge(bv_mm, 16 * (dc - 1))
                            e.tensor_copy(
                                out=wv_bf[dc % 2][:, hf * 4096:(hf + 1) * 4096],
                                in_=wv_f32[:]).then_inc(bv_cw, 1)

                @blk.tensor
                def _(e):
                    for dc in range(4):
                        e.wait_ge(bv_cw, 2 * (dc + 1))
                        for kt in range(16):
                            g = dc * 16 + kt
                            if g >= 2:
                                e.wait_ge(bv_ev, g - 1)
                            for ci in range(NCI):
                                mm = e.matmul(
                                    pp[g % 2][:],
                                    xn_fm[:, ci * S_OWN + kt * 128:
                                          ci * S_OWN + (kt + 1) * 128],
                                    wv_bf[dc % 2][:, ci * 512:(ci + 1) * 512],
                                    start=(ci == 0), stop=(ci == NCI - 1))
                                if ci == NCI - 1:
                                    mm.then_inc(bv_mm, 1)

                @blk.scalar
                def _(e):
                    for dc in range(4):
                        for kt in range(16):
                            g = dc * 16 + kt
                            e.wait_ge(bv_mm, g + 1)
                            if g >= 3:
                                e.wait_ge(bv_wr[g % 3], 16 * (g // 3))
                            e.activation(out=st3b[g % 3][:], in_=pp[g % 2][:],
                                         func=AF.Copy).then_inc(bv_ev, 1)
                            e.wait_ge(bv_ev, g + 1)
                            e.dma_start(
                                out=vtm_hl[dc][kt * 128:(kt + 1) * 128, :],
                                in_=st3b[g % 3][:]
                                ).then_inc(bv_wr[g % 3], 16)
                    for p in range(3):
                        tot = sum(1 for g in range(64) if g % 3 == p)
                        e.wait_ge(bv_wr[p], 16 * tot)

                @blk.gpsimd
                def _(e):
                    e.collective_compute(
                        "AllGather", ALU.bypass,
                        replica_groups=RG_PAIR,
                        ins=[kfm_h.opt()], outs=[kfm_f.opt()],
                    ).then_inc(cc_k, 1)
                    e.dma_start(out=bnd_d[0:1, :], in_=mq2[0:1, :]
                                ).then_inc(cc_t, 16)
                    e.dma_start(out=bnd_d[1:2, :], in_=mk2[0:1, :]
                                ).then_inc(cc_t, 16)
                    e.wait_ge(cc_t, 32)
                    e.collective_compute(
                        "AllReduce", ALU.max,
                        replica_groups=RG_PAIR,
                        ins=[bnd_d.opt()], outs=[bnd_g.opt()],
                    ).then_inc(cc_b, 1)
                    for dc in range(4):
                        for p in range(3):
                            cnt = sum(1 for g in range(16 * (dc + 1))
                                      if g % 3 == p)
                            e.wait_ge(bv_wr[p], 16 * cnt)
                        e.collective_compute(
                            "AllGather", ALU.bypass,
                            replica_groups=RG_PAIR,
                            ins=[vtm_hl[dc].opt()], outs=[vtm_fl[dc].opt()],
                        ).then_inc(cc_vd[dc], 1)
                    e.wait_ge(cc_b, 1)
                    e.dma_start(out=bnd_q[0:1, :], in_=bnd_g[0:1, :]
                                ).then_inc(cc_t, 16)
                    e.dma_start(out=bnd_k[0:1, :], in_=bnd_g[1:2, :]
                                ).then_inc(cc_t, 16)
                    e.wait_ge(cc_t, 64)
            b1b.close()
        ab.close()

        if "W" in phases:
            wjobs = []
            for r in range(NCI):
                for mat, img in ((gwT, gwb_d), (uwT, uwb_d)):
                    for q4 in range(4):
                        dsts = [img[q4 * 4 + s, :, r, :] for s in range(4)]
                        wjobs.append((mat[r * 128:(r + 1) * 128,
                                          q4 * 2048:(q4 + 1) * 2048], dsts))
            for r in range(FI // 128):
                qd, ic = r // 16, r % 16
                dsts = [dwb_d[hc, qd, :, ic, :] for hc in range(4)]
                wjobs.append((dwT[r * 128:(r + 1) * 128, :], dsts))
            NWJ = len(wjobs)
            NWGU = 128
            w_ld = [sem("w_ld0"), sem("w_ld1")]
            w_wr = [sem("w_wr0"), sem("w_wr1")]
            w_cv = sem("w_cv")

        if "W" in phases and "B2" not in phases:
            ws = ExitStack()
            wst_i32 = [ws.enter_context(
                nc.sbuf_tensor(f"wsti{i}", [128, 2048], I32)) for i in range(2)]
            wst_bf = [ws.enter_context(
                nc.sbuf_tensor(f"wstb{i}", [128, 2048], BF16)) for i in range(2)]

            with nc.Block("W") as blk:
                @blk.sync
                def _(e):
                    for j, (src, dsts) in enumerate(wjobs):
                        if j >= 2:
                            e.wait_ge(w_cv, j - 1)
                        e.dma_start(out=wst_i32[j % 2][:], in_=src
                                    ).then_inc(w_ld[j % 2], 16)

                @blk.vector
                def _(e):
                    for j in range(NWJ):
                        e.wait_ge(w_ld[j % 2], 16 * (j // 2 + 1))
                        if j >= 2:
                            e.wait_ge(w_wr[j % 2], 64 * (j // 2))
                        e.tensor_copy(out=wst_bf[j % 2][:],
                                      in_=wst_i32[j % 2][:]
                                      ).then_inc(w_cv, 1)

                @blk.scalar
                def _(e):
                    for j, (src, dsts) in enumerate(wjobs):
                        e.wait_ge(w_cv, j + 1)
                        for s, dst in enumerate(dsts):
                            e.dma_start(out=dst,
                                        in_=wst_bf[j % 2][:, s * 512:
                                                          (s + 1) * 512]
                                        ).then_inc(w_wr[j % 2], 16)
                    for p in range(2):
                        e.wait_ge(w_wr[p], 64 * (NWJ // 2))
            ws.close()

        if "B2" in phases:
            at = ExitStack()
            attnT = at.enter_context(
                nc.sbuf_tensor("attnT", [128, NH * S_OWN], BF16))

            b2s = ExitStack()
            qb = [b2s.enter_context(nc.sbuf_tensor(f"qb{i}", [128, S_OWN], BF16))
                  for i in range(2)]
            kb = [b2s.enter_context(nc.sbuf_tensor(f"kb{i}", [128, S_FULL], BF16))
                  for i in range(2)]
            vb4 = [b2s.enter_context(
                nc.sbuf_tensor(f"vb4_{i}", [128, 32 * 512], BF16))
                for i in range(2)]
            pt = [b2s.enter_context(nc.sbuf_tensor(f"pt{i}", [128, 1024], BF16))
                  for i in range(3)]
            invb = b2s.enter_context(nc.sbuf_tensor("invb", [128, 512], BF16))
            lrow = b2s.enter_context(nc.sbuf_tensor("lrow", [1, 512], F32))
            ilrow = b2s.enter_context(nc.sbuf_tensor("ilrow", [1, 512], F32))
            ilbf = b2s.enter_context(nc.sbuf_tensor("ilbf", [1, 512], BF16))
            bcol_all = b2s.enter_context(
                nc.sbuf_tensor("bcol_all", [128, NH], F32))
            btr = b2s.enter_context(nc.sbuf_tensor("btr", [1, NH], F32))
            btr2 = b2s.enter_context(nc.sbuf_tensor("btr2", [1, NH], F32))
            nbr = b2s.enter_context(nc.sbuf_tensor("nbr", [1, NH], BF16))
            w_in_b2 = "W" in phases
            if w_in_b2:
                wst_i32 = [b2s.enter_context(
                    nc.sbuf_tensor(f"wsti{i}", [128, 2048], I32))
                    for i in range(2)]
                wst_bf = [b2s.enter_context(
                    nc.sbuf_tensor(f"wstb{i}", [128, 2048], BF16))
                    for i in range(2)]
            pS = [b2s.enter_context(nc.psum_tensor(f"pS{i}", [128, 1024], F32))
                  for i in range(2)]
            paT = [b2s.enter_context(nc.psum_tensor(f"paT{i}", [128, 512], F32))
                   for i in range(2)]
            pmL = b2s.enter_context(nc.psum_tensor("pmL", [128, 512], F32))
            pm2 = b2s.enter_context(nc.psum_tensor("pm2", [128, 512], F32))

            b2_ld = [sem("b2_ld0"), sem("b2_ld1")]
            b2_vld = sem("b2_vld")
            b2_pre = sem("b2_pre")
            b2_smm = sem("b2_smm"); b2_exp = sem("b2_exp"); b2_pv = sem("b2_pv")
            b2_lpb = sem("b2_lpb"); b2_lrc = sem("b2_lrc")
            b2_lcp = sem("b2_lcp"); b2_bc = sem("b2_bc")
            b2_ibc = sem("b2_ibc"); b2_aev = sem("b2_aev")

            NB2 = 16
            with nc.Block("B2") as blk:
                @blk.sync
                def _(e):
                    def load_vb4(dg):
                        if dg >= 2:
                            e.wait_ge(b2_aev, 16 * (dg - 1))
                        e.wait_ge(cc_vd[dg], 1)
                        for kq in range(4):
                            r, kk = kq // 2, kq % 2
                            e.dma_start(
                                out=vb4[dg % 2][:, kq * 4096:(kq + 1) * 4096]
                                .rearrange("p (kt n) -> p kt n", n=512),
                                in_=vtm_fl[dg][r * 2048 + kk * 1024:
                                               r * 2048 + (kk + 1) * 1024, :]
                                .rearrange("(kt p) n -> p kt n", p=128)
                            ).then_inc(b2_vld, 16)

                    load_vb4(0)
                    for h in range(NH):
                        if h >= 2 and (h + 2) % 4 == 0 and (h + 2) // 4 < 4:
                            load_vb4((h + 2) // 4)
                        if h >= 2:
                            e.wait_ge(b2_aev, 4 * (h - 1))
                        e.dma_start(out=qb[h % 2][:], in_=qfm[h]
                                    ).then_inc(b2_ld[h % 2], 16)
                        if h == 0:
                            e.wait_ge(cc_k, 1)
                        for r in range(2):
                            e.dma_start(
                                out=kb[h % 2][:, r * 2048:(r + 1) * 2048],
                                in_=kfm_f[r * 2048 + h * 128:
                                          r * 2048 + (h + 1) * 128, :]
                            ).then_inc(b2_ld[h % 2], 16)

                @blk.vector
                def _(e):
                    e.wait_ge(cc_t, 64)
                    e.tensor_tensor(out=btr[0:1, :], in0=bnd_q[0:1, :],
                                    in1=bnd_k[0:1, :], op=ALU.mult
                                    ).then_inc(b2_pre, 1)
                    e.wait_ge(b2_pre, 2)
                    e.tensor_scalar(out=nbr[0:1, :], in0=btr2[0:1, :],
                                    scalar1=-1.0, scalar2=None, op0=ALU.mult
                                    ).then_inc(b2_pre, 1)
                    e.wait_ge(b2_pre, 4)
                    e.tensor_copy(out=bcol_all[:], in_=pm2[:, 0:NH]
                                  ).then_inc(b2_pre, 1)
                    for h in range(NH):
                        for qc in range(4):
                            hq = h * 4 + qc
                            e.wait_ge(b2_lpb, (hq + 1) * NB2)
                            e.tensor_copy(out=lrow[0:1, :], in_=pmL[0:1, 0:512]
                                          ).then_inc(b2_lrc, 1)
                            e.wait_ge(b2_lrc, hq + 1)
                            vmark(e.reciprocal(out=ilrow[0:1, :],
                                               in_=lrow[0:1, :]))
                            vwait(e)
                            e.tensor_copy(out=ilbf[0:1, :], in_=ilrow[0:1, :]
                                          ).then_inc(b2_lcp, 1)
                            e.wait_ge(b2_bc, hq + 1)
                            if hq >= 1:
                                e.wait_ge(b2_aev, hq)
                            e.tensor_copy(out=invb[:], in_=pm2[:, 0:512]
                                          ).then_inc(b2_ibc, 1)
                            e.wait_ge(b2_pv, (hq + 1) * NB2)
                            e.wait_ge(b2_ibc, hq + 1)
                            e.tensor_tensor(
                                out=attnT[:, h * S_OWN + qc * 512:
                                          h * S_OWN + qc * 512 + 512],
                                in0=paT[hq % 2][:], in1=invb[:], op=ALU.mult
                            ).then_inc(b2_aev, 1)
                            if w_in_b2:
                                for wc in range(3):
                                    j = hq * 3 + wc
                                    e.wait_ge(w_ld[j % 2], 16 * (j // 2 + 1))
                                    if j >= 2:
                                        e.wait_ge(w_wr[j % 2], 64 * (j // 2))
                                    e.tensor_copy(out=wst_bf[j % 2][:],
                                                  in_=wst_i32[j % 2][:]
                                                  ).then_inc(w_cv, 1)

                if w_in_b2:
                    @blk.gpsimd
                    def _(e):
                        def w_write(jp):
                            e.wait_ge(w_cv, jp + 1)
                            for s, dst in enumerate(wjobs[jp][1]):
                                e.dma_start(out=dst,
                                            in_=wst_bf[jp % 2][:, s * 512:
                                                              (s + 1) * 512]
                                            ).then_inc(w_wr[jp % 2], 16)

                        for j, (src, dsts) in enumerate(wjobs):
                            if j >= 2:
                                e.wait_ge(w_cv, j - 1)
                            e.dma_start(out=wst_i32[j % 2][:], in_=src
                                        ).then_inc(w_ld[j % 2], 16)
                            if j >= 2:
                                w_write(j - 2)
                        w_write(NWJ - 2)
                        w_write(NWJ - 1)

                @blk.scalar
                def _(e):
                    e.wait_ge(b2_pre, 1)
                    smark(e.activation(out=btr2[0:1, :], in_=btr[0:1, :],
                                       func=AF.Ln))
                    swait(e)
                    e.activation(out=btr2[0:1, :], in_=btr2[0:1, :],
                                 func=AF.Exp, scale=0.5).then_inc(b2_pre, 1)
                    e.wait_ge(b2_pre, 5)
                    for h in range(NH):
                        for qc in range(4):
                            hq = h * 4 + qc
                            for bi in range(NB2):
                                G = hq * NB2 + bi
                                e.wait_ge(b2_smm, G + 1)
                                if G >= 3:
                                    e.wait_ge(b2_pv, G - 2)
                                    e.wait_ge(b2_lpb, G - 2)
                                e.activation(out=pt[G % 3][:],
                                             in_=pS[G % 2][:],
                                             func=AF.Exp,
                                             bias=bcol_all[:, h:h + 1]
                                             ).then_inc(b2_exp, 1)

                @blk.tensor
                def _(e):
                    pend_bc = [None]

                    def emit_bc(phq):
                        e.wait_ge(b2_lcp, phq + 1)
                        e.wait_ge(b2_ibc, phq)
                        e.matmul(pm2[:, 0:512], ones_row1[0:1, :],
                                 ilbf[0:1, :], start=True, stop=True
                                 ).then_inc(b2_bc, 1)
                        pend_bc[0] = None

                    e.wait_ge(b2_pre, 3)
                    e.matmul(pm2[:, 0:NH], ones_row1[0:1, :], nbr[0:1, :],
                             start=True, stop=True).then_inc(b2_pre, 1)
                    for h in range(NH):
                        dg, hh = h // 4, h % 4
                        e.wait_ge(b2_ld[h % 2], 48 * (h // 2 + 1))
                        e.wait_ge(b2_vld, 64 * (dg + 1))
                        for qc in range(4):
                            hq = h * 4 + qc

                            def emit_S(bi):
                                G = hq * NB2 + bi
                                if G >= 2:
                                    e.wait_ge(b2_exp, G - 1)
                                for jj in range(2):
                                    kt = bi * 2 + jj
                                    mm = e.matmul(
                                        pS[G % 2][:, jj * 512:(jj + 1) * 512],
                                        kb[h % 2][:, kt * 128:(kt + 1) * 128],
                                        qb[h % 2][:, qc * 512:qc * 512 + 512],
                                        start=True, stop=True)
                                    if jj == 1:
                                        mm.then_inc(b2_smm, 1)

                            def emit_PVl(bi):
                                G = hq * NB2 + bi
                                e.wait_ge(b2_exp, G + 1)
                                if bi == 0 and hq >= 2:
                                    e.wait_ge(b2_aev, hq - 1)
                                for jj in range(2):
                                    kt = bi * 2 + jj
                                    mm = e.matmul(
                                        paT[hq % 2][:],
                                        vb4[dg % 2][:, kt * 512 + hh * 128:
                                                    kt * 512 + (hh + 1) * 128],
                                        pt[G % 3][:, jj * 512:(jj + 1) * 512],
                                        start=(kt == 0), stop=(kt == 31),
                                        skip_group_check=True)
                                    if jj == 1:
                                        mm.then_inc(b2_pv, 1)
                                if bi == 0 and hq >= 1:
                                    e.wait_ge(b2_lrc, hq)
                                for jj in range(2):
                                    kt = bi * 2 + jj
                                    mm = e.matmul(
                                        pmL[0:1, 0:512], ones_col[:, 0:1],
                                        pt[G % 3][:, jj * 512:(jj + 1) * 512],
                                        start=(kt == 0), stop=(kt == 31),
                                        skip_group_check=True)
                                    if jj == 1:
                                        mm.then_inc(b2_lpb, 1)

                            emit_S(0)
                            emit_S(1)
                            for bi in range(NB2):
                                emit_PVl(bi)
                                if bi == 3 and pend_bc[0] is not None:
                                    emit_bc(pend_bc[0])
                                if bi + 2 < NB2:
                                    emit_S(bi + 2)
                            pend_bc[0] = hq
                    emit_bc(NH * 4 - 1)
            b2s.close()

            if "B3" in phases:
                b3s = ExitStack()
                wo_f32 = b3s.enter_context(
                    nc.sbuf_tensor("wof", [128, FI], F32))
                wo_bf = b3s.enter_context(
                    nc.sbuf_tensor("wobf", [128, NH * D], BF16))
                s_st = [b3s.enter_context(
                    nc.sbuf_tensor(f"sst{i}", [128, 512], BF16)) for i in range(3)]
                ps3 = [b3s.enter_context(
                    nc.psum_tensor(f"ps3_{i}", [128, 512], F32)) for i in range(2)]
                b3_lw = sem("b3_lw"); b3_cw = sem("b3_cw"); b3_mm = sem("b3_mm")
                b3_ev = sem("b3_ev")
                b3_wr = [sem("b3_wr0"), sem("b3_wr1"), sem("b3_wr2")]
                if dbg:
                    b3_dbg = sem("b3_dbg")

                with nc.Block("B3") as blk:
                    @blk.sync
                    def _(e):
                        for q in range(4):
                            if q >= 1:
                                e.wait_ge(b3_cw, q)
                            e.dma_start(out=wo_f32[:],
                                        in_=wor[:, q * 4:(q + 1) * 4, :]
                                        .rearrange("p hh n -> p (hh n)")
                                        ).then_inc(b3_lw, 16)

                    @blk.vector
                    def _(e):
                        for q in range(4):
                            e.wait_ge(b3_lw, 16 * (q + 1))
                            e.tensor_copy(out=wo_bf[:, q * FI:(q + 1) * FI],
                                          in_=wo_f32[:]).then_inc(b3_cw, 1)

                    @blk.tensor
                    def _(e):
                        e.wait_ge(b3_cw, 4)
                        for qt in range(16):
                            for dc in range(4):
                                g = qt * 4 + dc
                                if g >= 2:
                                    e.wait_ge(b3_ev, g - 1)
                                for hh in range(NH):
                                    mm = e.matmul(
                                        ps3[g % 2][:],
                                        attnT[:, hh * S_OWN + qt * 128:
                                              hh * S_OWN + (qt + 1) * 128],
                                        wo_bf[:, hh * D + dc * 512:
                                              hh * D + dc * 512 + 512],
                                        start=(hh == 0), stop=(hh == NH - 1))
                                    if hh == NH - 1:
                                        mm.then_inc(b3_mm, 1)

                    @blk.scalar
                    def _(e):
                        for qt in range(16):
                            for dc in range(4):
                                g = qt * 4 + dc
                                e.wait_ge(b3_mm, g + 1)
                                if g >= 3:
                                    e.wait_ge(b3_wr[g % 3], 16 * (g // 3))
                                e.activation(out=s_st[g % 3][:], in_=ps3[g % 2][:],
                                             func=AF.Copy).then_inc(b3_ev, 1)
                                e.wait_ge(b3_ev, g + 1)
                                e.dma_start(
                                    out=stm[qt * 128:(qt + 1) * 128,
                                            dc * 512:dc * 512 + 512],
                                    in_=s_st[g % 3][:]
                                    ).then_inc(b3_wr[g % 3], 16)
                        for p in range(3):
                            tot = sum(1 for g in range(64) if g % 3 == p)
                            e.wait_ge(b3_wr[p], 16 * tot)

                    if dbg:
                        @blk.gpsimd
                        def _(e):
                            e.dma_start(out=atdbg, in_=attnT[:]
                                        ).then_inc(b3_dbg, 16)
                            e.wait_ge(b3_dbg, 16)
                b3s.close()
            at.close()

        if "C" in phases:
            cs = ExitStack()
            hT = cs.enter_context(nc.sbuf_tensor("hT", [128, 64 * 512], BF16))
            gw_bf = [cs.enter_context(nc.sbuf_tensor(f"gwsp{i}", [128, FI], BF16))
                     for i in range(2)]
            uw_bf = [cs.enter_context(nc.sbuf_tensor(f"uwsp{i}", [128, FI], BF16))
                     for i in range(2)]
            dwb = [cs.enter_context(nc.sbuf_tensor(f"dwb{i}", [128, FI], BF16))
                   for i in range(2)]
            xno = cs.enter_context(nc.sbuf_tensor("xno", [128, FI], BF16))
            stt_b = [cs.enter_context(nc.sbuf_tensor(f"sttb{i}", [128, 512], BF16))
                     for i in range(2)]
            dscb = cs.enter_context(nc.sbuf_tensor("dscb", [128, D], F32))
            xt = [cs.enter_context(nc.sbuf_tensor(f"xt{i}", [128, 512], F32))
                  for i in range(2)]
            sgb = [cs.enter_context(nc.sbuf_tensor(f"sgb{i}", [128, 512], F32))
                   for i in range(2)]
            utb = [cs.enter_context(nc.sbuf_tensor(f"utb{i}", [128, 512], F32))
                   for i in range(2)]
            outst = [cs.enter_context(nc.sbuf_tensor(f"outst{i}", [128, 512], F32))
                     for i in range(2)]
            ymb = sgb
            silb = utb
            rsneg = cs.enter_context(nc.sbuf_tensor("rsneg", [1, 512], BF16))
            hsneg = cs.enter_context(nc.sbuf_tensor("hsneg", [1, 512], BF16))
            c_dsc = sem("c_dsc")
            c_xno = sem("c_xno"); c_rs = sem("c_rs"); c_rse = sem("c_rse")
            c_lw = [sem("c_lw0"), sem("c_lw1")]
            c_mmg = sem("c_mmg"); c_mmu = sem("c_mmu")
            c_sg = sem("c_sg"); c_ut = sem("c_ut"); c_h = sem("c_h")
            d_sx = [sem("d_sx0"), sem("d_sx1")]
            d_xt = [sem("d_xt0"), sem("d_xt1")]
            d_lw = sem("d_lw")
            d_hs = sem("d_hs"); d_hse = sem("d_hse"); d_qd = sem("d_qd")
            d_mm = sem("d_mm"); d_ep1 = sem("d_ep1"); d_sil = sem("d_sil")
            d_ep3 = sem("d_ep3")
            d_wr = [sem("d_wr0"), sem("d_wr1")]

            for tc in range(4):
                gsec = ExitStack()
                pg = [gsec.enter_context(
                    nc.psum_tensor(f"pg{tc}_{i}", [128, 512], F32))
                    for i in range(2)]
                pu = [gsec.enter_context(
                    nc.psum_tensor(f"pu{tc}_{i}", [128, 512], F32))
                    for i in range(2)]
                pmc = gsec.enter_context(
                    nc.psum_tensor(f"pmcg{tc}", [128, 512], F32))

                with nc.Block(f"Cg{tc}") as blk:
                    @blk.sync
                    def _(e, tc=tc):
                        if tc == 0:
                            if "B2" in phases:
                                for p in range(2):
                                    e.wait_ge(w_wr[p], 64 * (NWGU // 2))
                            e.dma_start(out=dscb[:],
                                        in_=dscr.broadcast_to([128, D])
                                        ).then_inc(c_dsc, 16)
                        e.dma_start(out=xno[:],
                                    in_=xnof[tc].rearrange("p ci n -> p (ci n)")
                                    ).then_inc(c_xno, 16)
                        for sp in range(16):
                            if sp >= 2:
                                e.wait_ge(c_mmg, 64 * tc + 4 * (sp - 1))
                            e.dma_start(out=gw_bf[sp % 2][:],
                                        in_=gwb_d[sp]
                                        .rearrange("p ci n -> p (ci n)")
                                        ).then_inc(c_lw[sp % 2], 16)
                            if sp >= 2:
                                e.wait_ge(c_mmu, 64 * tc + 4 * (sp - 1))
                            e.dma_start(out=uw_bf[sp % 2][:],
                                        in_=uwb_d[sp]
                                        .rearrange("p ci n -> p (ci n)")
                                        ).then_inc(c_lw[sp % 2], 16)

                    @blk.tensor
                    def _(e, tc=tc, pg=pg, pu=pu, pmc=pmc):
                        e.wait_ge(c_xno, 16 * (tc + 1))
                        for ci in range(NCI):
                            mm = e.matmul(pmc[0:1, 0:512], ones_col[:, 0:1],
                                          xno[:, ci * 512:(ci + 1) * 512],
                                          start=(ci == 0), stop=(ci == NCI - 1))
                            if ci == NCI - 1:
                                mm.then_inc(c_rs, 1)
                        e.wait_ge(c_rse, tc + 1)
                        for it in range(64):
                            sp, itl = it // 4, it % 4
                            e.wait_ge(c_lw[sp % 2],
                                      256 * tc + 32 * (sp // 2 + 1))
                            if it >= 2:
                                e.wait_ge(c_sg, 64 * tc + it - 1)
                            for ci in range(NCI):
                                e.matmul(pg[it % 2][:],
                                         gw_bf[sp % 2][:, ci * 512 + itl * 128:
                                                       ci * 512 + (itl + 1) * 128],
                                         xno[:, ci * 512:(ci + 1) * 512],
                                         start=(ci == 0), stop=False)
                            e.matmul(pg[it % 2][:], ones_row1[0:1, :],
                                     rsneg[0:1, :], start=False, stop=True
                                     ).then_inc(c_mmg, 1)
                            if it >= 2:
                                e.wait_ge(c_ut, 64 * tc + it - 1)
                            for ci in range(NCI):
                                e.matmul(pu[it % 2][:],
                                         uw_bf[sp % 2][:, ci * 512 + itl * 128:
                                                       ci * 512 + (itl + 1) * 128],
                                         xno[:, ci * 512:(ci + 1) * 512],
                                         start=(ci == 0), stop=False)
                            e.matmul(pu[it % 2][:], ones_row1[0:1, :],
                                     rsneg[0:1, :], start=False, stop=True
                                     ).then_inc(c_mmu, 1)

                    @blk.scalar
                    def _(e, tc=tc, pg=pg, pu=pu, pmc=pmc):
                        e.wait_ge(c_rs, tc + 1)
                        e.activation(out=rsneg[0:1, :], in_=pmc[0:1, 0:512],
                                     func=AF.Copy, scale=-ZP).then_inc(c_rse, 1)
                        for it in range(64):
                            e.wait_ge(c_mmg, 64 * tc + it + 1)
                            if it >= 2:
                                e.wait_ge(c_h, 64 * tc + it - 1)
                            e.activation(out=sgb[it % 2][:], in_=pg[it % 2][:],
                                         func=AF.Silu, scale=gscol[:, it:it + 1]
                                         ).then_inc(c_sg, 1)
                            e.wait_ge(c_mmu, 64 * tc + it + 1)
                            e.activation(out=utb[it % 2][:], in_=pu[it % 2][:],
                                         func=AF.Identity,
                                         scale=uscol[:, it:it + 1]
                                         ).then_inc(c_ut, 1)

                    @blk.vector
                    def _(e, tc=tc):
                        for it in range(64):
                            e.wait_ge(c_sg, 64 * tc + it + 1)
                            e.wait_ge(c_ut, 64 * tc + it + 1)
                            e.tensor_tensor(out=hT[:, it * 512:(it + 1) * 512],
                                            in0=sgb[it % 2][:],
                                            in1=utb[it % 2][:],
                                            op=ALU.mult).then_inc(c_h, 1)
                gsec.close()

                dsec = ExitStack()
                py = [dsec.enter_context(
                    nc.psum_tensor(f"py{tc}_{i}", [128, 512], F32))
                    for i in range(4)]
                pmcd = dsec.enter_context(
                    nc.psum_tensor(f"pmcd{tc}", [128, 512], F32))

                with nc.Block(f"Cd{tc}") as blk:
                    if dbg and tc == 0:
                        @blk.gpsimd
                        def _(e, tc=tc, d_hse=d_hse):
                            dbs = nc.semaphore("dbgsem").__enter__()
                            e.dma_start(out=htdbg, in_=hT[:]).then_inc(dbs, 16)
                            e.wait_ge(d_hse, tc + 1)
                            e.dma_start(out=rowdbg[0:1, :], in_=rsneg[0:1, :]
                                        ).then_inc(dbs, 16)
                            e.dma_start(out=rowdbg[1:2, :], in_=hsneg[0:1, :]
                                        ).then_inc(dbs, 16)
                            e.wait_ge(dbs, 48)

                    @blk.sync
                    def _(e, tc=tc):
                        if tc == 0 and "B2" in phases:
                            for p in range(2):
                                e.wait_ge(w_wr[p], 64 * (NWJ // 2))
                        for hc in range(4):
                            for qd in range(4):
                                Q = hc * 4 + qd
                                if Q >= 2:
                                    e.wait_ge(d_qd, 16 * tc + Q - 1)
                                e.dma_start(out=dwb[Q % 2][:],
                                            in_=dwb_d[hc, qd]
                                            .rearrange("p ic n -> p (ic n)")
                                            ).then_inc(d_lw, 16)
                            for tt in range(4):
                                k = hc * 4 + tt
                                if k >= 2:
                                    e.wait_ge(d_ep3, 16 * tc + k - 1)
                                e.dma_start(
                                    out=xt[k % 2][:],
                                    in_=x[tc * 512 + tt * 128:
                                          tc * 512 + (tt + 1) * 128,
                                          hc * 512:(hc + 1) * 512]
                                ).then_inc(d_xt[k % 2], 16)
                                e.dma_start(
                                    out=stt_b[k % 2][:],
                                    in_=stm[tc * 512 + tt * 128:
                                            tc * 512 + (tt + 1) * 128,
                                            hc * 512:(hc + 1) * 512]
                                ).then_inc(d_sx[k % 2], 16)

                    @blk.tensor
                    def _(e, tc=tc, py=py, pmcd=pmcd):
                        for ic in range(64):
                            mm = e.matmul(pmcd[0:1, 0:512], ones_col[:, 0:1],
                                          hT[:, ic * 512:(ic + 1) * 512],
                                          start=(ic == 0), stop=(ic == 63))
                            if ic == 63:
                                mm.then_inc(d_hs, 1)
                        for hc in range(4):
                            for qd in range(4):
                                Q = hc * 4 + qd
                                e.wait_ge(d_lw, 256 * tc + 16 * (Q + 1))
                                if qd == 0 and hc >= 1:
                                    e.wait_ge(d_ep1, 16 * tc + 4 * hc)
                                for ic16 in range(16):
                                    ic = qd * 16 + ic16
                                    for tt in range(4):
                                        mm = e.matmul(
                                            py[tt][:],
                                            hT[:, ic * 512 + tt * 128:
                                               ic * 512 + (tt + 1) * 128],
                                            dwb[Q % 2][:, ic16 * 512:
                                                       (ic16 + 1) * 512],
                                            start=(qd == 0 and ic16 == 0),
                                            stop=False, skip_group_check=True)
                                        if ic16 == 15 and tt == 3:
                                            mm.then_inc(d_qd, 1)
                            e.wait_ge(d_hse, tc + 1)
                            for tt in range(4):
                                e.matmul(py[tt][:],
                                         hsneg[0:1, tt * 128:(tt + 1) * 128],
                                         ones_row512[0:1, :], start=False,
                                         stop=True, skip_group_check=True
                                         ).then_inc(d_mm, 1)

                    @blk.scalar
                    def _(e, tc=tc, pmcd=pmcd):
                        e.wait_ge(d_hs, tc + 1)
                        e.activation(out=hsneg[0:1, :], in_=pmcd[0:1, 0:512],
                                     func=AF.Copy, scale=-ZP).then_inc(d_hse, 1)
                        for k in range(16):
                            hc, tt = k // 4, k % 4
                            e.wait_ge(d_ep1, 16 * tc + k + 1)
                            if k >= 2:
                                e.wait_ge(d_ep3, 16 * tc + k - 1)
                            e.activation(out=silb[k % 2][:], in_=ymb[k % 2][:],
                                         func=AF.Silu).then_inc(d_sil, 1)
                            e.wait_ge(d_ep3, 16 * tc + k + 1)
                            e.dma_start(
                                out=out[tc * 512 + tt * 128:
                                        tc * 512 + (tt + 1) * 128,
                                        hc * 512:(hc + 1) * 512],
                                in_=outst[k % 2][:]).then_inc(d_wr[k % 2], 16)
                        e.wait_ge(d_wr[0], 128 * (tc + 1))
                        e.wait_ge(d_wr[1], 128 * (tc + 1))

                    @blk.vector
                    def _(e, tc=tc, py=py):
                        if tc == 0:
                            e.wait_ge(c_dsc, 16)
                        for k in range(16):
                            hc, tt = k // 4, k % 4
                            e.wait_ge(d_mm, 16 * tc + k + 1)
                            if k >= 2:
                                e.wait_ge(d_sil, 16 * tc + k - 1)
                                e.wait_ge(d_ep3, 16 * tc + k - 1)
                            e.scalar_tensor_tensor(
                                out=ymb[k % 2][:], in0=py[tt][:], scalar=1.0,
                                in1=dscb[:, hc * 512:(hc + 1) * 512],
                                op0=ALU.mult, op1=ALU.mult).then_inc(d_ep1, 1)
                            e.wait_ge(d_sil, 16 * tc + k + 1)
                            e.wait_ge(d_xt[k % 2],
                                      128 * tc + 16 * (k // 2 + 1))
                            e.wait_ge(d_sx[k % 2],
                                      128 * tc + 16 * (k // 2 + 1))
                            if k >= 2:
                                e.wait_ge(d_wr[k % 2],
                                          128 * tc + 16 * (k // 2))
                            vmark(e.tensor_tensor(
                                out=ymb[k % 2][:], in0=silb[k % 2][:],
                                in1=stt_b[k % 2][:], op=ALU.mult))
                            vwait(e)
                            e.tensor_tensor(
                                out=outst[k % 2][:], in0=ymb[k % 2][:],
                                in1=xt[k % 2][:], op=ALU.add).then_inc(d_ep3, 1)
                dsec.close()
            cs.close()
    return nc


def _host_arrange(Wq, Wk, Wv, Wo, gate_w, up_w, down_w, gate_scale, up_scale,
                  down_scale):
    Wq = np.asarray(Wq, dtype=np.float32)
    Wk = np.asarray(Wk, dtype=np.float32)
    Wv = np.asarray(Wv, dtype=np.float32)
    Wo = np.asarray(Wo, dtype=np.float32)
    wqr = np.ascontiguousarray(
        Wq.reshape(NH, 128, NCI, 128).transpose(0, 3, 2, 1).reshape(NH, 128, D))
    wkr = np.ascontiguousarray(
        Wk.reshape(NH, 128, NCI, 128).transpose(0, 3, 2, 1).reshape(NH, 128, D))
    wvr = np.ascontiguousarray(
        Wv.reshape(4, 512, NCI, 128).transpose(0, 3, 2, 1).reshape(4, 128, FI))
    wor = np.ascontiguousarray(
        Wo.reshape(D, NH, 128).transpose(2, 1, 0))
    gwT = np.ascontiguousarray(np.asarray(gate_w, dtype=np.int32).T)
    uwT = np.ascontiguousarray(np.asarray(up_w, dtype=np.int32).T)
    dwT = np.ascontiguousarray(np.asarray(down_w, dtype=np.int32).T)
    gsc = np.ascontiguousarray(
        np.asarray(gate_scale, dtype=np.float32).reshape(64, 128))
    usc = np.ascontiguousarray(
        np.asarray(up_scale, dtype=np.float32).reshape(64, 128))
    dscr = np.ascontiguousarray(
        np.asarray(down_scale, dtype=np.float32).reshape(1, D))
    eye = np.eye(128, dtype=np.float32)
    return dict(wqr=wqr, wkr=wkr, wvr=wvr, wor=wor, gwT=gwT, uwT=uwT, dwT=dwT,
                gsc=gsc, usc=usc, dscr=dscr, eye=eye)


def kernel(L0_attn_hidden_states, L0_attn_Wq, L0_attn_Wk, L0_attn_Wv,
           L0_attn_Wo, L0_ffn_gate_w, L0_ffn_gate_scale, L0_ffn_up_w,
           L0_ffn_up_scale, L0_ffn_down_w, L0_ffn_down_scale):
    x = np.ascontiguousarray(np.asarray(L0_attn_hidden_states, dtype=np.float32))
    shared = _host_arrange(L0_attn_Wq, L0_attn_Wk, L0_attn_Wv, L0_attn_Wo,
                           L0_ffn_gate_w, L0_ffn_up_w, L0_ffn_down_w,
                           L0_ffn_gate_scale, L0_ffn_up_scale,
                           L0_ffn_down_scale)
    in_maps = []
    for c in range(8):
        b, h2 = c // 2, c % 2
        xb = x[b]
        if h2 == 0:
            xc = xb
        else:
            xc = np.ascontiguousarray(
                np.concatenate([xb[S_OWN:], xb[:S_OWN]], axis=0))
        m = dict(shared)
        m["x"] = xc
        in_maps.append(m)

    nc = build_program(dbg=DBG)
    res = run_bass_kernel_spmd(nc, in_maps, core_ids=list(range(8)))
    outp = np.empty((4, S_FULL, D), dtype=np.float32)
    for c in range(8):
        b, h2 = c // 2, c % 2
        outp[b, h2 * S_OWN:(h2 + 1) * S_OWN, :] = np.asarray(
            res.results[c]["out"], dtype=np.float32)
    return outp


# version 09
# speedup vs baseline: 1.1738x; 1.1738x over previous
import numpy as np
from contextlib import ExitStack

import concourse.bass as bass
import concourse.mybir as mybir
from concourse.bass_utils import run_bass_kernel_spmd

F32 = mybir.dt.float32
BF16 = mybir.dt.bfloat16
I32 = mybir.dt.int32
AF = mybir.ActivationFunctionType
ALU = mybir.AluOpType
AX = mybir.AxisListType

D = 2048
S_FULL = 4096
S_OWN = 2048
NH = 16
DH = 128
FI = 8192
NCI = D // 128
SCALE = DH ** -0.5
ZP = 128.0

DBG = False


def build_program(dbg=False, stop_after=None):
    phases = ["A", "B1", "W", "B2", "B3", "C"]
    if stop_after is not None:
        phases = phases[: phases.index(stop_after) + 1]

    nc = bass.Bass("TRN2", target_bir_lowering=False)

    x = nc.dram_tensor("x", [S_FULL, D], F32, kind="ExternalInput").ap()
    wqr = nc.dram_tensor("wqr", [NH, 128, D], F32, kind="ExternalInput").ap()
    wkr = nc.dram_tensor("wkr", [NH, 128, D], F32, kind="ExternalInput").ap()
    wvr = nc.dram_tensor("wvr", [4, 128, FI], F32, kind="ExternalInput").ap()
    wor = nc.dram_tensor("wor", [128, NH, D], F32, kind="ExternalInput").ap()
    gwT = nc.dram_tensor("gwT", [D, FI], I32, kind="ExternalInput").ap()
    uwT = nc.dram_tensor("uwT", [D, FI], I32, kind="ExternalInput").ap()
    dwT = nc.dram_tensor("dwT", [FI, D], I32, kind="ExternalInput").ap()
    gsc = nc.dram_tensor("gsc", [64, 128], F32, kind="ExternalInput").ap()
    usc = nc.dram_tensor("usc", [64, 128], F32, kind="ExternalInput").ap()
    dscr = nc.dram_tensor("dscr", [1, D], F32, kind="ExternalInput").ap()
    eye = nc.dram_tensor("eye", [128, 128], F32, kind="ExternalInput").ap()
    out = nc.dram_tensor("out", [S_OWN, D], F32, kind="ExternalOutput").ap()

    sk = "ExternalOutput" if dbg else "Internal"
    xnof = nc.dram_tensor("xnof", [4, 128, NCI, 512], BF16, kind=sk).ap()
    kfm_h = nc.dram_tensor("kfm_h", [NH * 128, S_OWN], BF16, kind=sk).ap()
    kfm_f = nc.dram_tensor("kfm_f", [2 * NH * 128, S_OWN], BF16,
                           kind="Internal").ap()
    qfm = nc.dram_tensor("qfm", [NH, DH, S_OWN], BF16, kind=sk).ap()
    vtm_hl = [nc.dram_tensor(f"vtm_h{dc}", [S_OWN, 512], BF16, kind=sk).ap()
              for dc in range(4)]
    vtm_fl = [nc.dram_tensor(f"vtm_f{dc}", [2 * S_OWN, 512], BF16,
                             kind="Internal").ap() for dc in range(4)]
    bnd_d = nc.dram_tensor("bnd_d", [2, NH], F32, kind="Internal").ap()
    bnd_g = nc.dram_tensor("bnd_g", [2, NH], F32, kind="Internal").ap()
    stm = nc.dram_tensor("stm", [S_OWN, D], BF16, kind=sk).ap()
    gwb_d = nc.dram_tensor("gwb_d", [16, 128, NCI, 512], BF16, kind="Internal").ap()
    uwb_d = nc.dram_tensor("uwb_d", [16, 128, NCI, 512], BF16, kind="Internal").ap()
    dwb_d = nc.dram_tensor("dwb_d", [4, 4, 128, 16, 512], BF16,
                           kind=("ExternalOutput" if dbg else "Internal")).ap()
    atdbg = None
    if dbg:
        atdbg = nc.dram_tensor("atdbg", [128, NH * S_OWN], BF16,
                               kind="ExternalOutput").ap()
        htdbg = nc.dram_tensor("htdbg", [128, 64 * 512], BF16,
                               kind="ExternalOutput").ap()
        rowdbg = nc.dram_tensor("rowdbg", [2, 512], BF16,
                                kind="ExternalOutput").ap()
        l0dbg = nc.dram_tensor("l0dbg", [4, 512], F32,
                               kind="ExternalOutput").ap()
        bc0dbg = nc.dram_tensor("bc0dbg", [128, 1], F32,
                                kind="ExternalOutput").ap()

    es = ExitStack()

    def sem(name):
        return es.enter_context(nc.semaphore(name))

    with es:
        vss = es.enter_context(nc.semaphore("vss")); vssn = [0]
        sss = es.enter_context(nc.semaphore("sss")); sssn = [0]
        pss = es.enter_context(nc.semaphore("pss")); pssn = [0]

        def vmark(instr):
            vssn[0] += 1
            instr.then_inc(vss, 1)

        def vwait(e):
            if vssn[0]:
                e.wait_ge(vss, vssn[0])

        def smark(instr):
            sssn[0] += 1
            instr.then_inc(sss, 1)

        def swait(e):
            if sssn[0]:
                e.wait_ge(sss, sssn[0])

        def pmark(instr):
            pssn[0] += 1
            instr.then_inc(pss, 1)

        def pwait(e):
            if pssn[0]:
                e.wait_ge(pss, pssn[0])

        ones_col = es.enter_context(nc.sbuf_tensor("ones_col", [128, 1], BF16))
        ones_row1 = es.enter_context(nc.sbuf_tensor("ones_row1", [1, 128], BF16))
        ones_row512 = es.enter_context(
            nc.sbuf_tensor("ones_row512", [1, 512], BF16))
        epscol = es.enter_context(nc.sbuf_tensor("epscol", [128, 1], F32))
        mq2 = es.enter_context(nc.sbuf_tensor("mq2", [1, NH], F32))
        mk2 = es.enter_context(nc.sbuf_tensor("mk2", [1, NH], F32))
        bnd_q = es.enter_context(nc.sbuf_tensor("bnd_q", [1, NH], F32))
        bnd_k = es.enter_context(nc.sbuf_tensor("bnd_k", [1, NH], F32))
        gscol = es.enter_context(nc.sbuf_tensor("gscol", [128, 64], F32))
        uscol = es.enter_context(nc.sbuf_tensor("uscol", [128, 64], F32))
        ntmp = es.enter_context(nc.sbuf_tensor("ntmp", [1, 2], F32))

        p0_ld = sem("p0_ld")
        with nc.Block("P0") as blk:
            @blk.sync
            def _(e):
                n = 0
                for j in range(64):
                    e.dma_start(out=gscol[:, j:j + 1],
                                in_=gsc[j:j + 1, :].rearrange("o p -> p o")
                                ).then_inc(p0_ld, 16); n += 1
                    e.dma_start(out=uscol[:, j:j + 1],
                                in_=usc[j:j + 1, :].rearrange("o p -> p o")
                                ).then_inc(p0_ld, 16); n += 1
                e.wait_ge(p0_ld, 16 * n)

            @blk.vector
            def _(e):
                e.memset(ones_col[:], 1.0)
                e.memset(ones_row1[:], 1.0)
                e.memset(ones_row512[:], 1.0)
                e.memset(epscol[:], 1e-5)
                e.memset(mq2[:], 0.0)
                e.memset(mk2[:], 0.0)

        ab = ExitStack()
        xn_fm = ab.enter_context(nc.sbuf_tensor("xn_fm", [128, NCI * S_OWN],
                                                BF16))
        pp = [ab.enter_context(nc.psum_tensor(f"pp{i}", [128, 512], F32))
              for i in range(2)]
        pmisc = ab.enter_context(nc.psum_tensor("pmisc", [128, 512], F32))

        aa = ExitStack()
        seye = aa.enter_context(nc.sbuf_tensor("seye", [128, 128], F32))
        eye_bf = aa.enter_context(nc.sbuf_tensor("eye_bf", [128, 128], BF16))
        xbuf = [aa.enter_context(nc.sbuf_tensor(f"xbuf{i}", [128, D], F32))
                for i in range(2)]
        xn_tm = [aa.enter_context(nc.sbuf_tensor(f"xn_tm{i}", [128, D], BF16))
                 for i in range(2)]
        colsA = [aa.enter_context(nc.sbuf_tensor(f"colsA{i}", [128, 8], F32))
                 for i in range(2)]
        sqscr = aa.enter_context(nc.sbuf_tensor("sqscr", [128, D], F32))
        ptp = [aa.enter_context(nc.psum_tensor(f"ptp{i}", [128, 128], BF16))
               for i in range(2)]
        a_eld = sem("a_eld")
        a_ld = [sem("a_ld0"), sem("a_ld1")]
        a_sq = sem("a_sq"); a_st = sem("a_st")
        a_rs = sem("a_rs"); a_nb = sem("a_nb"); a_xn = sem("a_xn")
        a_tp = sem("a_tp"); a_ev = sem("a_ev"); a_wb = sem("a_wb")
        NT_A = S_OWN // 128

        with nc.Block("A") as blk:
            @blk.sync
            def _(e):
                e.dma_start(out=seye[:], in_=eye).then_inc(a_eld, 16)
                for i in range(NT_A):
                    if i >= 2:
                        e.wait_ge(a_xn, i - 1)
                    e.dma_start(out=xbuf[i % 2][:],
                                in_=x[i * 128:(i + 1) * 128, :]
                                ).then_inc(a_ld[i % 2], 16)

            @blk.scalar
            def _(e):
                for i in range(NT_A):
                    e.wait_ge(a_ld[i % 2], 16 * (i // 2 + 1))
                    if i >= 2:
                        e.wait_ge(a_st, i - 1)
                    e.activation(out=sqscr[:], in_=xbuf[i % 2][:], func=AF.Square,
                                 accum_out=colsA[i % 2][:, 2:3]).then_inc(a_sq, 1)
                    e.wait_ge(a_st, i + 1)
                    smark(e.activation(out=colsA[i % 2][:, 6:7],
                                       in_=colsA[i % 2][:, 5:6],
                                       func=AF.Ln, bias=epscol[:, 0:1]))
                    swait(e)
                    e.activation(out=colsA[i % 2][:, 7:8], in_=colsA[i % 2][:, 6:7],
                                 func=AF.Exp, scale=-0.5).then_inc(a_rs, 1)

            @blk.vector
            def _(e):
                e.wait_ge(a_eld, 16)
                e.tensor_copy(out=eye_bf[:], in_=seye[:])
                for i in range(NT_A):
                    c = colsA[i % 2]
                    e.wait_ge(a_ld[i % 2], 16 * (i // 2 + 1))
                    if i >= 2:
                        e.wait_ge(a_xn, i - 1)
                    vmark(e.reduce_sum(out=c[:, 0:1], in_=xbuf[i % 2][:],
                                       axis=AX.X, negate=True))
                    vwait(e)
                    vmark(e.tensor_scalar(out=c[:, 1:2], in0=c[:, 0:1],
                                          scalar1=1.0 / D, scalar2=None,
                                          op0=ALU.mult))
                    e.wait_ge(a_sq, i + 1)
                    vmark(e.tensor_scalar(out=c[:, 3:4], in0=c[:, 2:3],
                                          scalar1=1.0 / D, scalar2=None,
                                          op0=ALU.mult))
                    vwait(e)
                    vmark(e.tensor_tensor(out=c[:, 4:5], in0=c[:, 1:2],
                                          in1=c[:, 1:2], op=ALU.mult))
                    vwait(e)
                    e.tensor_tensor(out=c[:, 5:6], in0=c[:, 3:4], in1=c[:, 4:5],
                                    op=ALU.subtract).then_inc(a_st, 1)
                    e.wait_ge(a_rs, i + 1)
                    vmark(e.tensor_scalar(out=c[:, 4:5], in0=c[:, 1:2],
                                          scalar1=c[:, 7:8], scalar2=None,
                                          op0=ALU.mult))
                    if i >= 2:
                        e.wait_ge(a_tp, 16 * (i - 1))
                    vwait(e)
                    e.tensor_scalar(out=xn_tm[i % 2][:], in0=xbuf[i % 2][:],
                                    scalar1=c[:, 7:8], scalar2=c[:, 4:5],
                                    op0=ALU.mult, op1=ALU.add).then_inc(a_xn, 1)
                    for ci in range(NCI):
                        g = i * NCI + ci
                        e.wait_ge(a_tp, g + 1)
                        e.tensor_copy(
                            out=xn_fm[:, ci * S_OWN + i * 128:
                                      ci * S_OWN + (i + 1) * 128],
                            in_=ptp[g % 2][:]).then_inc(a_ev, 1)

            @blk.tensor
            def _(e):
                for i in range(NT_A):
                    e.wait_ge(a_xn, i + 1)
                    for ci in range(NCI):
                        g = i * NCI + ci
                        if g >= 2:
                            e.wait_ge(a_ev, g - 1)
                        e.transpose(ptp[g % 2][:],
                                    xn_tm[i % 2][:, ci * 128:(ci + 1) * 128],
                                    eye_bf[:]).then_inc(a_tp, 1)

            @blk.gpsimd
            def _(e):
                e.wait_ge(a_ev, NT_A * NCI)
                for tc in range(4):
                    for ci in range(NCI):
                        e.dma_start(
                            out=xnof[tc, :, ci, :],
                            in_=xn_fm[:, ci * S_OWN + tc * 512:
                                      ci * S_OWN + tc * 512 + 512]
                            ).then_inc(a_wb, 16)
                e.wait_ge(a_wb, 16 * 64)
        aa.close()

        if "B1" in phases:
            b1a = ExitStack()
            wqk_f32 = [b1a.enter_context(
                nc.sbuf_tensor(f"wqkf{i}", [128, D], F32)) for i in range(2)]
            wq_bf = [b1a.enter_context(
                nc.sbuf_tensor(f"wqbf{i}", [128, D], BF16)) for i in range(2)]
            wk_bf = [b1a.enter_context(
                nc.sbuf_tensor(f"wkbf{i}", [128, D], BF16)) for i in range(2)]
            st3 = [b1a.enter_context(
                nc.sbuf_tensor(f"st3_{i}", [128, 512], BF16)) for i in range(3)]
            nsq = [b1a.enter_context(
                nc.sbuf_tensor(f"nsq{i}", [128, 512], BF16)) for i in range(2)]

            b1_lwq = sem("b1_lwq"); b1_lwk = sem("b1_lwk")
            b1_cwq = sem("b1_cwq"); b1_cwk = sem("b1_cwk")
            b1_mm = sem("b1_mm"); b1_ev = sem("b1_ev")
            b1_wr = [sem("b1_wr0"), sem("b1_wr1"), sem("b1_wr2")]
            b1_sq = sem("b1_sq"); b1_ns = sem("b1_ns"); b1_nm = sem("b1_nm")
            NG = 8

            with nc.Block("B1a") as blk:
                @blk.sync
                def _(e):
                    for h in range(NH):
                        if h >= 1:
                            e.wait_ge(b1_cwq, h)
                        e.dma_start(out=wqk_f32[0][:], in_=wqr[h]
                                    ).then_inc(b1_lwq, 16)
                        if h >= 1:
                            e.wait_ge(b1_cwk, h)
                        e.dma_start(out=wqk_f32[1][:], in_=wkr[h]
                                    ).then_inc(b1_lwk, 16)

                @blk.vector
                def _(e):
                    for h in range(NH):
                        e.wait_ge(b1_lwq, 16 * (h + 1))
                        if h >= 2:
                            e.wait_ge(b1_mm, NG * (h - 1))
                        e.tensor_copy(out=wq_bf[h % 2][:], in_=wqk_f32[0][:]
                                      ).then_inc(b1_cwq, 1)
                        e.wait_ge(b1_lwk, 16 * (h + 1))
                        e.tensor_copy(out=wk_bf[h % 2][:], in_=wqk_f32[1][:]
                                      ).then_inc(b1_cwk, 1)
                        for j in range(NG):
                            g = h * NG + j
                            e.wait_ge(b1_ev, g + 1)
                            if g >= 2:
                                e.wait_ge(b1_ns, g - 1)
                            e.tensor_tensor(out=nsq[g % 2][:], in0=st3[g % 3][:],
                                            in1=st3[g % 3][:], op=ALU.mult
                                            ).then_inc(b1_sq, 1)
                            e.wait_ge(b1_ns, g + 1)
                            dst = mq2 if j < 4 else mk2
                            vmark(e.reduce_max(out=ntmp[0:1, 0:1],
                                               in_=pmisc[0:1, 0:512], axis=AX.X))
                            vwait(e)
                            e.tensor_tensor(out=dst[0:1, h:h + 1],
                                            in0=dst[0:1, h:h + 1],
                                            in1=ntmp[0:1, 0:1], op=ALU.max
                                            ).then_inc(b1_nm, 1)

                @blk.tensor
                def _(e):
                    def colsum(gp):
                        e.wait_ge(b1_sq, gp + 1)
                        if gp >= 1:
                            e.wait_ge(b1_nm, gp)
                        e.matmul(pmisc[0:1, 0:512], ones_col[:, 0:1],
                                 nsq[gp % 2][:], start=True, stop=True
                                 ).then_inc(b1_ns, 1)

                    for h in range(NH):
                        for j in range(NG):
                            g = h * NG + j
                            isq = j < 4
                            wbf = wq_bf[h % 2] if isq else wk_bf[h % 2]
                            tc = j if isq else j - 4
                            if isq and j == 0:
                                e.wait_ge(b1_cwq, h + 1)
                            if (not isq) and j == 4:
                                e.wait_ge(b1_cwk, h + 1)
                            if g >= 2:
                                e.wait_ge(b1_ev, g - 1)
                            for ci in range(NCI):
                                mm = e.matmul(
                                    pp[g % 2][:],
                                    wbf[:, ci * 128:(ci + 1) * 128],
                                    xn_fm[:, ci * S_OWN + tc * 512:
                                          ci * S_OWN + tc * 512 + 512],
                                    start=(ci == 0), stop=(ci == NCI - 1))
                                if ci == NCI - 1:
                                    mm.then_inc(b1_mm, 1)
                            if j >= 1:
                                colsum(g - 1)
                            if j == NG - 1:
                                colsum(g)

                @blk.scalar
                def _(e):
                    for h in range(NH):
                        for j in range(NG):
                            g = h * NG + j
                            isq = j < 4
                            tc = j if isq else j - 4
                            e.wait_ge(b1_mm, g + 1)
                            if g >= 3:
                                e.wait_ge(b1_wr[g % 3], 16 * (g // 3))
                                e.wait_ge(b1_sq, g - 2)
                            e.activation(out=st3[g % 3][:], in_=pp[g % 2][:],
                                         func=AF.Copy,
                                         scale=(SCALE if isq else 1.0)
                                         ).then_inc(b1_ev, 1)
                            e.wait_ge(b1_ev, g + 1)
                            if isq:
                                dst = qfm[h, :, tc * 512:tc * 512 + 512]
                            else:
                                dst = kfm_h[h * 128:(h + 1) * 128,
                                            tc * 512:tc * 512 + 512]
                            e.dma_start(out=dst, in_=st3[g % 3][:]
                                        ).then_inc(b1_wr[g % 3], 16)
                    for p in range(3):
                        tot = sum(1 for g in range(NH * NG) if g % 3 == p)
                        e.wait_ge(b1_wr[p], 16 * tot)
            b1a.close()

            b1b = ExitStack()
            wv_f32 = b1b.enter_context(nc.sbuf_tensor("wvf", [128, 4096], F32))
            wv_bf = [b1b.enter_context(
                nc.sbuf_tensor(f"wvbf{i}", [128, FI], BF16)) for i in range(2)]
            st3b = [b1b.enter_context(
                nc.sbuf_tensor(f"st3b{i}", [128, 512], BF16)) for i in range(3)]
            bv_lw = sem("bv_lw"); bv_cw = sem("bv_cw"); bv_mm = sem("bv_mm")
            bv_ev = sem("bv_ev")
            bv_wr = [sem("bv_wr0"), sem("bv_wr1"), sem("bv_wr2")]
            cc_k = sem("cc_k"); cc_b = sem("cc_b"); cc_t = sem("cc_t")
            cc_vd = [sem(f"cc_vd{dc}") for dc in range(4)]
            RG_PAIR = [[2 * i, 2 * i + 1] for i in range(4)]

            with nc.Block("B1b") as blk:
                @blk.sync
                def _(e):
                    for dc in range(4):
                        for hf in range(2):
                            u = dc * 2 + hf
                            if u >= 1:
                                e.wait_ge(bv_cw, u)
                            e.dma_start(out=wv_f32[:],
                                        in_=wvr[dc, :, hf * 4096:(hf + 1) * 4096]
                                        ).then_inc(bv_lw, 16)

                @blk.vector
                def _(e):
                    for dc in range(4):
                        for hf in range(2):
                            u = dc * 2 + hf
                            e.wait_ge(bv_lw, 16 * (u + 1))
                            if dc >= 2 and hf == 0:
                                e.wait_ge(bv_mm, 16 * (dc - 1))
                            e.tensor_copy(
                                out=wv_bf[dc % 2][:, hf * 4096:(hf + 1) * 4096],
                                in_=wv_f32[:]).then_inc(bv_cw, 1)

                @blk.tensor
                def _(e):
                    for dc in range(4):
                        e.wait_ge(bv_cw, 2 * (dc + 1))
                        for kt in range(16):
                            g = dc * 16 + kt
                            if g >= 2:
                                e.wait_ge(bv_ev, g - 1)
                            for ci in range(NCI):
                                mm = e.matmul(
                                    pp[g % 2][:],
                                    xn_fm[:, ci * S_OWN + kt * 128:
                                          ci * S_OWN + (kt + 1) * 128],
                                    wv_bf[dc % 2][:, ci * 512:(ci + 1) * 512],
                                    start=(ci == 0), stop=(ci == NCI - 1))
                                if ci == NCI - 1:
                                    mm.then_inc(bv_mm, 1)

                @blk.scalar
                def _(e):
                    for dc in range(4):
                        for kt in range(16):
                            g = dc * 16 + kt
                            e.wait_ge(bv_mm, g + 1)
                            if g >= 3:
                                e.wait_ge(bv_wr[g % 3], 16 * (g // 3))
                            e.activation(out=st3b[g % 3][:], in_=pp[g % 2][:],
                                         func=AF.Copy).then_inc(bv_ev, 1)
                            e.wait_ge(bv_ev, g + 1)
                            e.dma_start(
                                out=vtm_hl[dc][kt * 128:(kt + 1) * 128, :],
                                in_=st3b[g % 3][:]
                                ).then_inc(bv_wr[g % 3], 16)
                    for p in range(3):
                        tot = sum(1 for g in range(64) if g % 3 == p)
                        e.wait_ge(bv_wr[p], 16 * tot)

                @blk.gpsimd
                def _(e):
                    e.collective_compute(
                        "AllGather", ALU.bypass,
                        replica_groups=RG_PAIR,
                        ins=[kfm_h.opt()], outs=[kfm_f.opt()],
                    ).then_inc(cc_k, 1)
                    e.dma_start(out=bnd_d[0:1, :], in_=mq2[0:1, :]
                                ).then_inc(cc_t, 16)
                    e.dma_start(out=bnd_d[1:2, :], in_=mk2[0:1, :]
                                ).then_inc(cc_t, 16)
                    e.wait_ge(cc_t, 32)
                    e.collective_compute(
                        "AllReduce", ALU.max,
                        replica_groups=RG_PAIR,
                        ins=[bnd_d.opt()], outs=[bnd_g.opt()],
                    ).then_inc(cc_b, 1)
                    for dc in range(4):
                        for p in range(3):
                            cnt = sum(1 for g in range(16 * (dc + 1))
                                      if g % 3 == p)
                            e.wait_ge(bv_wr[p], 16 * cnt)
                        e.collective_compute(
                            "AllGather", ALU.bypass,
                            replica_groups=RG_PAIR,
                            ins=[vtm_hl[dc].opt()], outs=[vtm_fl[dc].opt()],
                        ).then_inc(cc_vd[dc], 1)
                    e.wait_ge(cc_b, 1)
                    e.dma_start(out=bnd_q[0:1, :], in_=bnd_g[0:1, :]
                                ).then_inc(cc_t, 16)
                    e.dma_start(out=bnd_k[0:1, :], in_=bnd_g[1:2, :]
                                ).then_inc(cc_t, 16)
                    e.wait_ge(cc_t, 64)
            b1b.close()
        ab.close()

        if "W" in phases:
            wjobs = []
            for r in range(NCI):
                for mat, img in ((gwT, gwb_d), (uwT, uwb_d)):
                    for q4 in range(4):
                        dsts = [img[q4 * 4 + s, :, r, :] for s in range(4)]
                        wjobs.append((mat[r * 128:(r + 1) * 128,
                                          q4 * 2048:(q4 + 1) * 2048], dsts))
            for r in range(FI // 128):
                qd, ic = r // 16, r % 16
                dsts = [dwb_d[hc, qd, :, ic, :] for hc in range(4)]
                wjobs.append((dwT[r * 128:(r + 1) * 128, :], dsts))
            NWJ = len(wjobs)
            NWGU = 128
            w_ld = [sem("w_ld0"), sem("w_ld1")]
            w_wr = [sem("w_wr0"), sem("w_wr1")]
            w_cv = sem("w_cv")

        if "W" in phases and "B2" not in phases:
            ws = ExitStack()
            wst_i32 = [ws.enter_context(
                nc.sbuf_tensor(f"wsti{i}", [128, 2048], I32)) for i in range(2)]
            wst_bf = [ws.enter_context(
                nc.sbuf_tensor(f"wstb{i}", [128, 2048], BF16)) for i in range(2)]

            with nc.Block("W") as blk:
                @blk.sync
                def _(e):
                    for j, (src, dsts) in enumerate(wjobs):
                        if j >= 2:
                            e.wait_ge(w_cv, j - 1)
                        e.dma_start(out=wst_i32[j % 2][:], in_=src
                                    ).then_inc(w_ld[j % 2], 16)

                @blk.vector
                def _(e):
                    for j in range(NWJ):
                        e.wait_ge(w_ld[j % 2], 16 * (j // 2 + 1))
                        if j >= 2:
                            e.wait_ge(w_wr[j % 2], 64 * (j // 2))
                        e.tensor_copy(out=wst_bf[j % 2][:],
                                      in_=wst_i32[j % 2][:]
                                      ).then_inc(w_cv, 1)

                @blk.scalar
                def _(e):
                    for j, (src, dsts) in enumerate(wjobs):
                        e.wait_ge(w_cv, j + 1)
                        for s, dst in enumerate(dsts):
                            e.dma_start(out=dst,
                                        in_=wst_bf[j % 2][:, s * 512:
                                                          (s + 1) * 512]
                                        ).then_inc(w_wr[j % 2], 16)
                    for p in range(2):
                        e.wait_ge(w_wr[p], 64 * (NWJ // 2))
            ws.close()

        if "B2" in phases:
            at = ExitStack()
            attnT = at.enter_context(
                nc.sbuf_tensor("attnT", [128, NH * S_OWN], BF16))

            b2s = ExitStack()
            qb = [b2s.enter_context(nc.sbuf_tensor(f"qb{i}", [128, S_OWN], BF16))
                  for i in range(2)]
            kb = [b2s.enter_context(nc.sbuf_tensor(f"kb{i}", [128, S_FULL], BF16))
                  for i in range(2)]
            vb4 = [b2s.enter_context(
                nc.sbuf_tensor(f"vb4_{i}", [128, 32 * 512], BF16))
                for i in range(2)]
            pt = [b2s.enter_context(nc.sbuf_tensor(f"pt{i}", [128, 1024], BF16))
                  for i in range(3)]
            invb = b2s.enter_context(nc.sbuf_tensor("invb", [128, 512], BF16))
            lrow = b2s.enter_context(nc.sbuf_tensor("lrow", [1, 512], F32))
            ilrow = b2s.enter_context(nc.sbuf_tensor("ilrow", [1, 512], F32))
            ilbf = b2s.enter_context(nc.sbuf_tensor("ilbf", [1, 512], BF16))
            bcol_all = b2s.enter_context(
                nc.sbuf_tensor("bcol_all", [128, NH], F32))
            btr = b2s.enter_context(nc.sbuf_tensor("btr", [1, NH], F32))
            btr2 = b2s.enter_context(nc.sbuf_tensor("btr2", [1, NH], F32))
            nbr = b2s.enter_context(nc.sbuf_tensor("nbr", [1, NH], BF16))
            w_in_b2 = "W" in phases
            if w_in_b2:
                wst_i32 = [b2s.enter_context(
                    nc.sbuf_tensor(f"wsti{i}", [128, 2048], I32))
                    for i in range(2)]
                wst_bf = [b2s.enter_context(
                    nc.sbuf_tensor(f"wstb{i}", [128, 2048], BF16))
                    for i in range(2)]
            pS = [b2s.enter_context(nc.psum_tensor(f"pS{i}", [128, 1024], F32))
                  for i in range(2)]
            paT = [b2s.enter_context(nc.psum_tensor(f"paT{i}", [128, 512], F32))
                   for i in range(2)]
            pmL = b2s.enter_context(nc.psum_tensor("pmL", [128, 512], F32))
            pm2 = b2s.enter_context(nc.psum_tensor("pm2", [128, 512], F32))

            b2_ld = [sem("b2_ld0"), sem("b2_ld1")]
            b2_vld = sem("b2_vld"); b2_vldg = sem("b2_vldg")
            b2_pre = sem("b2_pre")
            b2_smm = sem("b2_smm"); b2_exp = sem("b2_exp"); b2_pv = sem("b2_pv")
            b2_lpb = sem("b2_lpb"); b2_lrc = sem("b2_lrc")
            b2_lcp = sem("b2_lcp"); b2_bc = sem("b2_bc")
            b2_ibc = sem("b2_ibc"); b2_aev = sem("b2_aev")

            NB2 = 16
            with nc.Block("B2") as blk:
                @blk.sync
                def _(e):
                    def load_vb4(dg, kqs=(0, 1, 2, 3)):
                        if dg >= 2:
                            e.wait_ge(b2_aev, 16 * (dg - 1))
                        e.wait_ge(cc_vd[dg], 1)
                        for kq in kqs:
                            r, kk = kq // 2, kq % 2
                            e.dma_start(
                                out=vb4[dg % 2][:, kq * 4096:(kq + 1) * 4096]
                                .rearrange("p (kt n) -> p kt n", n=512),
                                in_=vtm_fl[dg][r * 2048 + kk * 1024:
                                               r * 2048 + (kk + 1) * 1024, :]
                                .rearrange("(kt p) n -> p kt n", p=128)
                            ).then_inc(b2_vld, 16)

                    for h in range(NH):
                        if h == 2:
                            load_vb4(0, kqs=(0, 1))
                        if h >= 2 and (h + 2) % 4 == 0 and (h + 2) // 4 < 4:
                            load_vb4((h + 2) // 4)
                        if h >= 2:
                            e.wait_ge(b2_aev, 4 * (h - 1))
                        e.dma_start(out=qb[h % 2][:], in_=qfm[h]
                                    ).then_inc(b2_ld[h % 2], 16)
                        if h == 0:
                            e.wait_ge(cc_k, 1)
                        for r in range(2):
                            e.dma_start(
                                out=kb[h % 2][:, r * 2048:(r + 1) * 2048],
                                in_=kfm_f[r * 2048 + h * 128:
                                          r * 2048 + (h + 1) * 128, :]
                            ).then_inc(b2_ld[h % 2], 16)

                @blk.vector
                def _(e):
                    e.wait_ge(cc_t, 64)
                    e.tensor_tensor(out=btr[0:1, :], in0=bnd_q[0:1, :],
                                    in1=bnd_k[0:1, :], op=ALU.mult
                                    ).then_inc(b2_pre, 1)
                    e.wait_ge(b2_pre, 2)
                    e.tensor_scalar(out=nbr[0:1, :], in0=btr2[0:1, :],
                                    scalar1=-1.0, scalar2=None, op0=ALU.mult
                                    ).then_inc(b2_pre, 1)
                    e.wait_ge(b2_pre, 4)
                    e.tensor_copy(out=bcol_all[:], in_=pm2[:, 0:NH]
                                  ).then_inc(b2_pre, 1)
                    for h in range(NH):
                        for qc in range(4):
                            hq = h * 4 + qc
                            e.wait_ge(b2_lpb, (hq + 1) * NB2)
                            e.tensor_copy(out=lrow[0:1, :], in_=pmL[0:1, 0:512]
                                          ).then_inc(b2_lrc, 1)
                            e.wait_ge(b2_lrc, hq + 1)
                            vmark(e.reciprocal(out=ilrow[0:1, :],
                                               in_=lrow[0:1, :]))
                            vwait(e)
                            e.tensor_copy(out=ilbf[0:1, :], in_=ilrow[0:1, :]
                                          ).then_inc(b2_lcp, 1)
                            e.wait_ge(b2_bc, hq + 1)
                            if hq >= 1:
                                e.wait_ge(b2_aev, hq)
                            e.tensor_copy(out=invb[:], in_=pm2[:, 0:512]
                                          ).then_inc(b2_ibc, 1)
                            e.wait_ge(b2_pv, (hq + 1) * NB2)
                            e.wait_ge(b2_ibc, hq + 1)
                            e.tensor_tensor(
                                out=attnT[:, h * S_OWN + qc * 512:
                                          h * S_OWN + qc * 512 + 512],
                                in0=paT[hq % 2][:], in1=invb[:], op=ALU.mult
                            ).then_inc(b2_aev, 1)
                            if w_in_b2:
                                for wc in range(3):
                                    j = hq * 3 + wc
                                    e.wait_ge(w_ld[j % 2], 16 * (j // 2 + 1))
                                    if j >= 2:
                                        e.wait_ge(w_wr[j % 2], 64 * (j // 2))
                                    e.tensor_copy(out=wst_bf[j % 2][:],
                                                  in_=wst_i32[j % 2][:]
                                                  ).then_inc(w_cv, 1)

                if w_in_b2:
                    @blk.gpsimd
                    def _(e):
                        e.wait_ge(cc_vd[0], 1)
                        for kq in (2, 3):
                            r, kk = kq // 2, kq % 2
                            e.dma_start(
                                out=vb4[0][:, kq * 4096:(kq + 1) * 4096]
                                .rearrange("p (kt n) -> p kt n", n=512),
                                in_=vtm_fl[0][r * 2048 + kk * 1024:
                                              r * 2048 + (kk + 1) * 1024, :]
                                .rearrange("(kt p) n -> p kt n", p=128)
                            ).then_inc(b2_vldg, 16)

                        def w_write(jp):
                            e.wait_ge(w_cv, jp + 1)
                            for s, dst in enumerate(wjobs[jp][1]):
                                e.dma_start(out=dst,
                                            in_=wst_bf[jp % 2][:, s * 512:
                                                              (s + 1) * 512]
                                            ).then_inc(w_wr[jp % 2], 16)

                        for j, (src, dsts) in enumerate(wjobs):
                            if j >= 2:
                                e.wait_ge(w_cv, j - 1)
                            e.dma_start(out=wst_i32[j % 2][:], in_=src
                                        ).then_inc(w_ld[j % 2], 16)
                            if j >= 2:
                                w_write(j - 2)
                        w_write(NWJ - 2)
                        w_write(NWJ - 1)

                @blk.scalar
                def _(e):
                    e.wait_ge(b2_pre, 1)
                    smark(e.activation(out=btr2[0:1, :], in_=btr[0:1, :],
                                       func=AF.Ln))
                    swait(e)
                    e.activation(out=btr2[0:1, :], in_=btr2[0:1, :],
                                 func=AF.Exp, scale=0.5).then_inc(b2_pre, 1)
                    e.wait_ge(b2_pre, 5)
                    for h in range(NH):
                        for qc in range(4):
                            hq = h * 4 + qc
                            for bi in range(NB2):
                                G = hq * NB2 + bi
                                e.wait_ge(b2_smm, G + 1)
                                if G >= 3:
                                    e.wait_ge(b2_pv, G - 2)
                                    e.wait_ge(b2_lpb, G - 2)
                                e.activation(out=pt[G % 3][:],
                                             in_=pS[G % 2][:],
                                             func=AF.Exp,
                                             bias=bcol_all[:, h:h + 1]
                                             ).then_inc(b2_exp, 1)

                @blk.tensor
                def _(e):
                    pend_bc = [None]

                    def emit_bc(phq):
                        e.wait_ge(b2_lcp, phq + 1)
                        e.wait_ge(b2_ibc, phq)
                        e.matmul(pm2[:, 0:512], ones_row1[0:1, :],
                                 ilbf[0:1, :], start=True, stop=True
                                 ).then_inc(b2_bc, 1)
                        pend_bc[0] = None

                    e.wait_ge(b2_pre, 3)
                    e.matmul(pm2[:, 0:NH], ones_row1[0:1, :], nbr[0:1, :],
                             start=True, stop=True).then_inc(b2_pre, 1)
                    for h in range(NH):
                        dg, hh = h // 4, h % 4
                        e.wait_ge(b2_ld[h % 2], 48 * (h // 2 + 1))
                        if dg == 0:
                            e.wait_ge(b2_vld, 32)
                            e.wait_ge(b2_vldg, 32)
                        else:
                            e.wait_ge(b2_vld, 32 + 64 * dg)
                        for qc in range(4):
                            hq = h * 4 + qc

                            def emit_S(bi):
                                G = hq * NB2 + bi
                                if G >= 2:
                                    e.wait_ge(b2_exp, G - 1)
                                for jj in range(2):
                                    kt = bi * 2 + jj
                                    mm = e.matmul(
                                        pS[G % 2][:, jj * 512:(jj + 1) * 512],
                                        kb[h % 2][:, kt * 128:(kt + 1) * 128],
                                        qb[h % 2][:, qc * 512:qc * 512 + 512],
                                        start=True, stop=True)
                                    if jj == 1:
                                        mm.then_inc(b2_smm, 1)

                            def emit_PVl(bi):
                                G = hq * NB2 + bi
                                e.wait_ge(b2_exp, G + 1)
                                if bi == 0 and hq >= 2:
                                    e.wait_ge(b2_aev, hq - 1)
                                for jj in range(2):
                                    kt = bi * 2 + jj
                                    mm = e.matmul(
                                        paT[hq % 2][:],
                                        vb4[dg % 2][:, kt * 512 + hh * 128:
                                                    kt * 512 + (hh + 1) * 128],
                                        pt[G % 3][:, jj * 512:(jj + 1) * 512],
                                        start=(kt == 0), stop=(kt == 31),
                                        skip_group_check=True)
                                    if jj == 1:
                                        mm.then_inc(b2_pv, 1)
                                if bi == 0 and hq >= 1:
                                    e.wait_ge(b2_lrc, hq)
                                for jj in range(2):
                                    kt = bi * 2 + jj
                                    mm = e.matmul(
                                        pmL[0:1, 0:512], ones_col[:, 0:1],
                                        pt[G % 3][:, jj * 512:(jj + 1) * 512],
                                        start=(kt == 0), stop=(kt == 31),
                                        skip_group_check=True)
                                    if jj == 1:
                                        mm.then_inc(b2_lpb, 1)

                            emit_S(0)
                            emit_S(1)
                            for bi in range(NB2):
                                emit_PVl(bi)
                                if bi == 3 and pend_bc[0] is not None:
                                    emit_bc(pend_bc[0])
                                if bi + 2 < NB2:
                                    emit_S(bi + 2)
                            pend_bc[0] = hq
                    emit_bc(NH * 4 - 1)
            b2s.close()

            if "B3" in phases:
                b3s = ExitStack()
                wo_f32 = b3s.enter_context(
                    nc.sbuf_tensor("wof", [128, FI], F32))
                wo_bf = b3s.enter_context(
                    nc.sbuf_tensor("wobf", [128, NH * D], BF16))
                s_st = [b3s.enter_context(
                    nc.sbuf_tensor(f"sst{i}", [128, 512], BF16)) for i in range(3)]
                ps3 = [b3s.enter_context(
                    nc.psum_tensor(f"ps3_{i}", [128, 512], F32)) for i in range(2)]
                b3_lw = sem("b3_lw"); b3_cw = sem("b3_cw"); b3_mm = sem("b3_mm")
                b3_ev = sem("b3_ev")
                b3_wr = [sem("b3_wr0"), sem("b3_wr1"), sem("b3_wr2")]
                if dbg:
                    b3_dbg = sem("b3_dbg")

                with nc.Block("B3") as blk:
                    @blk.sync
                    def _(e):
                        for q in range(4):
                            if q >= 1:
                                e.wait_ge(b3_cw, q)
                            e.dma_start(out=wo_f32[:],
                                        in_=wor[:, q * 4:(q + 1) * 4, :]
                                        .rearrange("p hh n -> p (hh n)")
                                        ).then_inc(b3_lw, 16)


                    @blk.vector
                    def _(e):
                        for q in range(4):
                            e.wait_ge(b3_lw, 16 * (q + 1))
                            e.tensor_copy(out=wo_bf[:, q * FI:(q + 1) * FI],
                                          in_=wo_f32[:]).then_inc(b3_cw, 1)

                    @blk.tensor
                    def _(e):
                        e.wait_ge(b3_cw, 4)
                        for qt in range(16):
                            for dc in range(4):
                                g = qt * 4 + dc
                                if g >= 2:
                                    e.wait_ge(b3_ev, g - 1)
                                for hh in range(NH):
                                    mm = e.matmul(
                                        ps3[g % 2][:],
                                        attnT[:, hh * S_OWN + qt * 128:
                                              hh * S_OWN + (qt + 1) * 128],
                                        wo_bf[:, hh * D + dc * 512:
                                              hh * D + dc * 512 + 512],
                                        start=(hh == 0), stop=(hh == NH - 1))
                                    if hh == NH - 1:
                                        mm.then_inc(b3_mm, 1)

                    @blk.scalar
                    def _(e):
                        for qt in range(16):
                            for dc in range(4):
                                g = qt * 4 + dc
                                e.wait_ge(b3_mm, g + 1)
                                if g >= 3:
                                    e.wait_ge(b3_wr[g % 3], 16 * (g // 3))
                                e.activation(out=s_st[g % 3][:], in_=ps3[g % 2][:],
                                             func=AF.Copy).then_inc(b3_ev, 1)
                                e.wait_ge(b3_ev, g + 1)
                                e.dma_start(
                                    out=stm[qt * 128:(qt + 1) * 128,
                                            dc * 512:dc * 512 + 512],
                                    in_=s_st[g % 3][:]
                                    ).then_inc(b3_wr[g % 3], 16)
                        for p in range(3):
                            tot = sum(1 for g in range(64) if g % 3 == p)
                            e.wait_ge(b3_wr[p], 16 * tot)

                    if dbg:
                        @blk.gpsimd
                        def _(e):
                            e.dma_start(out=atdbg, in_=attnT[:]
                                        ).then_inc(b3_dbg, 16)
                            e.wait_ge(b3_dbg, 16)
                b3s.close()
            at.close()

        if "C" in phases:
            cs = ExitStack()
            hT = cs.enter_context(nc.sbuf_tensor("hT", [128, 64 * 512], BF16))
            gw_bf = [cs.enter_context(nc.sbuf_tensor(f"gwsp{i}", [128, FI], BF16))
                     for i in range(2)]
            uw_bf = [cs.enter_context(nc.sbuf_tensor(f"uwsp{i}", [128, FI], BF16))
                     for i in range(2)]
            dwb = [cs.enter_context(nc.sbuf_tensor(f"dwb{i}", [128, FI], BF16))
                   for i in range(2)]
            xno = cs.enter_context(nc.sbuf_tensor("xno", [128, FI], BF16))
            stt_b = [cs.enter_context(nc.sbuf_tensor(f"sttb{i}", [128, 512], BF16))
                     for i in range(2)]
            dscb = cs.enter_context(nc.sbuf_tensor("dscb", [128, D], F32))
            xt = [cs.enter_context(nc.sbuf_tensor(f"xt{i}", [128, 512], F32))
                  for i in range(2)]
            sgb = [cs.enter_context(nc.sbuf_tensor(f"sgb{i}", [128, 512], F32))
                   for i in range(2)]
            utb = [cs.enter_context(nc.sbuf_tensor(f"utb{i}", [128, 512], F32))
                   for i in range(2)]
            outst = [cs.enter_context(nc.sbuf_tensor(f"outst{i}", [128, 512], F32))
                     for i in range(2)]
            ymb = sgb
            silb = utb
            rsneg = cs.enter_context(nc.sbuf_tensor("rsneg", [1, 512], BF16))
            hsneg = cs.enter_context(nc.sbuf_tensor("hsneg", [1, 512], BF16))
            c_dsc = sem("c_dsc")
            c_xno = sem("c_xno"); c_rs = sem("c_rs"); c_rse = sem("c_rse")
            c_lw = [sem("c_lw0"), sem("c_lw1")]
            c_mmg = sem("c_mmg"); c_mmu = sem("c_mmu")
            c_sg = sem("c_sg"); c_ut = sem("c_ut"); c_h = sem("c_h")
            d_sx = [sem("d_sx0"), sem("d_sx1")]
            d_xt = [sem("d_xt0"), sem("d_xt1")]
            d_lw = sem("d_lw")
            d_hs = sem("d_hs"); d_hse = sem("d_hse"); d_qd = sem("d_qd")
            d_mm = sem("d_mm"); d_ep1 = sem("d_ep1"); d_sil = sem("d_sil")
            d_ep3 = sem("d_ep3")
            d_wr = [sem("d_wr0"), sem("d_wr1")]

            for tc in range(4):
                gsec = ExitStack()
                pg = [gsec.enter_context(
                    nc.psum_tensor(f"pg{tc}_{i}", [128, 512], F32))
                    for i in range(2)]
                pu = [gsec.enter_context(
                    nc.psum_tensor(f"pu{tc}_{i}", [128, 512], F32))
                    for i in range(2)]
                pmc = gsec.enter_context(
                    nc.psum_tensor(f"pmcg{tc}", [128, 512], F32))

                with nc.Block(f"Cg{tc}") as blk:
                    @blk.sync
                    def _(e, tc=tc):
                        if tc == 0:
                            if "B2" in phases:
                                for p in range(2):
                                    e.wait_ge(w_wr[p], 64 * (NWGU // 2))
                            e.dma_start(out=dscb[:],
                                        in_=dscr.broadcast_to([128, D])
                                        ).then_inc(c_dsc, 16)
                        e.dma_start(out=xno[:],
                                    in_=xnof[tc].rearrange("p ci n -> p (ci n)")
                                    ).then_inc(c_xno, 16)
                        for sp in range(16):
                            if sp < 2 and tc >= 1:
                                continue
                            if sp >= 2:
                                e.wait_ge(c_mmg, 64 * tc + 4 * (sp - 1))
                            e.dma_start(out=gw_bf[sp % 2][:],
                                        in_=gwb_d[sp]
                                        .rearrange("p ci n -> p (ci n)")
                                        ).then_inc(c_lw[sp % 2], 16)
                            if sp >= 2:
                                e.wait_ge(c_mmu, 64 * tc + 4 * (sp - 1))
                            e.dma_start(out=uw_bf[sp % 2][:],
                                        in_=uwb_d[sp]
                                        .rearrange("p ci n -> p (ci n)")
                                        ).then_inc(c_lw[sp % 2], 16)
                        if tc == 0 and "B2" in phases:
                            for p in range(2):
                                e.wait_ge(w_wr[p], 64 * (NWJ // 2))
                        for qd in range(2):
                            e.dma_start(out=dwb[qd % 2][:],
                                        in_=dwb_d[0, qd]
                                        .rearrange("p ic n -> p (ic n)")
                                        ).then_inc(d_lw, 16)

                    @blk.tensor
                    def _(e, tc=tc, pg=pg, pu=pu, pmc=pmc):
                        e.wait_ge(c_xno, 16 * (tc + 1))
                        for ci in range(NCI):
                            mm = e.matmul(pmc[0:1, 0:512], ones_col[:, 0:1],
                                          xno[:, ci * 512:(ci + 1) * 512],
                                          start=(ci == 0), stop=(ci == NCI - 1))
                            if ci == NCI - 1:
                                mm.then_inc(c_rs, 1)
                        e.wait_ge(c_rse, tc + 1)
                        for it in range(64):
                            sp, itl = it // 4, it % 4
                            e.wait_ge(c_lw[sp % 2],
                                      256 * tc + 32 * (sp // 2 + 1))
                            if it >= 2:
                                e.wait_ge(c_sg, 64 * tc + it - 1)
                            for ci in range(NCI):
                                e.matmul(pg[it % 2][:],
                                         gw_bf[sp % 2][:, ci * 512 + itl * 128:
                                                       ci * 512 + (itl + 1) * 128],
                                         xno[:, ci * 512:(ci + 1) * 512],
                                         start=(ci == 0), stop=False)
                            e.matmul(pg[it % 2][:], ones_row1[0:1, :],
                                     rsneg[0:1, :], start=False, stop=True
                                     ).then_inc(c_mmg, 1)
                            if it >= 2:
                                e.wait_ge(c_ut, 64 * tc + it - 1)
                            for ci in range(NCI):
                                e.matmul(pu[it % 2][:],
                                         uw_bf[sp % 2][:, ci * 512 + itl * 128:
                                                       ci * 512 + (itl + 1) * 128],
                                         xno[:, ci * 512:(ci + 1) * 512],
                                         start=(ci == 0), stop=False)
                            e.matmul(pu[it % 2][:], ones_row1[0:1, :],
                                     rsneg[0:1, :], start=False, stop=True
                                     ).then_inc(c_mmu, 1)

                    @blk.scalar
                    def _(e, tc=tc, pg=pg, pu=pu, pmc=pmc):
                        e.wait_ge(c_rs, tc + 1)
                        e.activation(out=rsneg[0:1, :], in_=pmc[0:1, 0:512],
                                     func=AF.Copy, scale=-ZP).then_inc(c_rse, 1)
                        for it in range(64):
                            e.wait_ge(c_mmg, 64 * tc + it + 1)
                            if it >= 2:
                                e.wait_ge(c_h, 64 * tc + it - 1)
                            e.activation(out=sgb[it % 2][:], in_=pg[it % 2][:],
                                         func=AF.Silu, scale=gscol[:, it:it + 1]
                                         ).then_inc(c_sg, 1)
                            e.wait_ge(c_mmu, 64 * tc + it + 1)
                            e.activation(out=utb[it % 2][:], in_=pu[it % 2][:],
                                         func=AF.Identity,
                                         scale=uscol[:, it:it + 1]
                                         ).then_inc(c_ut, 1)

                    @blk.vector
                    def _(e, tc=tc):
                        for it in range(64):
                            e.wait_ge(c_sg, 64 * tc + it + 1)
                            e.wait_ge(c_ut, 64 * tc + it + 1)
                            e.tensor_tensor(out=hT[:, it * 512:(it + 1) * 512],
                                            in0=sgb[it % 2][:],
                                            in1=utb[it % 2][:],
                                            op=ALU.mult).then_inc(c_h, 1)
                gsec.close()

                dsec = ExitStack()
                py = [dsec.enter_context(
                    nc.psum_tensor(f"py{tc}_{i}", [128, 512], F32))
                    for i in range(4)]
                pmcd = dsec.enter_context(
                    nc.psum_tensor(f"pmcd{tc}", [128, 512], F32))

                with nc.Block(f"Cd{tc}") as blk:
                    if dbg and tc == 0:
                        @blk.gpsimd
                        def _(e, tc=tc, d_hse=d_hse):
                            dbs = nc.semaphore("dbgsem").__enter__()
                            e.dma_start(out=htdbg, in_=hT[:]).then_inc(dbs, 16)
                            e.wait_ge(d_hse, tc + 1)
                            e.dma_start(out=rowdbg[0:1, :], in_=rsneg[0:1, :]
                                        ).then_inc(dbs, 16)
                            e.dma_start(out=rowdbg[1:2, :], in_=hsneg[0:1, :]
                                        ).then_inc(dbs, 16)
                            e.wait_ge(dbs, 48)

                    @blk.sync
                    def _(e, tc=tc):
                        for hc in range(4):
                            for qd in range(4):
                                Q = hc * 4 + qd
                                if Q < 2:
                                    continue
                                e.wait_ge(d_qd, 16 * tc + Q - 1)
                                e.dma_start(out=dwb[Q % 2][:],
                                            in_=dwb_d[hc, qd]
                                            .rearrange("p ic n -> p (ic n)")
                                            ).then_inc(d_lw, 16)
                            for tt in range(4):
                                k = hc * 4 + tt
                                if k >= 2:
                                    e.wait_ge(d_ep3, 16 * tc + k - 1)
                                e.dma_start(
                                    out=xt[k % 2][:],
                                    in_=x[tc * 512 + tt * 128:
                                          tc * 512 + (tt + 1) * 128,
                                          hc * 512:(hc + 1) * 512]
                                ).then_inc(d_xt[k % 2], 16)
                                e.dma_start(
                                    out=stt_b[k % 2][:],
                                    in_=stm[tc * 512 + tt * 128:
                                            tc * 512 + (tt + 1) * 128,
                                            hc * 512:(hc + 1) * 512]
                                ).then_inc(d_sx[k % 2], 16)
                        if tc < 3:
                            for sp in range(2):
                                e.wait_ge(c_mmg, 64 * (tc + 1) + 4 * (sp - 1))
                                e.dma_start(out=gw_bf[sp % 2][:],
                                            in_=gwb_d[sp]
                                            .rearrange("p ci n -> p (ci n)")
                                            ).then_inc(c_lw[sp % 2], 16)
                                e.wait_ge(c_mmu, 64 * (tc + 1) + 4 * (sp - 1))
                                e.dma_start(out=uw_bf[sp % 2][:],
                                            in_=uwb_d[sp]
                                            .rearrange("p ci n -> p (ci n)")
                                            ).then_inc(c_lw[sp % 2], 16)

                    @blk.tensor
                    def _(e, tc=tc, py=py, pmcd=pmcd):
                        for ic in range(64):
                            mm = e.matmul(pmcd[0:1, 0:512], ones_col[:, 0:1],
                                          hT[:, ic * 512:(ic + 1) * 512],
                                          start=(ic == 0), stop=(ic == 63))
                            if ic == 63:
                                mm.then_inc(d_hs, 1)
                        for hc in range(4):
                            for qd in range(4):
                                Q = hc * 4 + qd
                                e.wait_ge(d_lw, 256 * tc + 16 * (Q + 1))
                                if qd == 0 and hc >= 1:
                                    e.wait_ge(d_ep1, 16 * tc + 4 * hc)
                                for ic16 in range(16):
                                    ic = qd * 16 + ic16
                                    for tt in range(4):
                                        mm = e.matmul(
                                            py[tt][:],
                                            hT[:, ic * 512 + tt * 128:
                                               ic * 512 + (tt + 1) * 128],
                                            dwb[Q % 2][:, ic16 * 512:
                                                       (ic16 + 1) * 512],
                                            start=(qd == 0 and ic16 == 0),
                                            stop=False, skip_group_check=True)
                                        if ic16 == 15 and tt == 3:
                                            mm.then_inc(d_qd, 1)
                            e.wait_ge(d_hse, tc + 1)
                            for tt in range(4):
                                e.matmul(py[tt][:],
                                         hsneg[0:1, tt * 128:(tt + 1) * 128],
                                         ones_row512[0:1, :], start=False,
                                         stop=True, skip_group_check=True
                                         ).then_inc(d_mm, 1)

                    @blk.scalar
                    def _(e, tc=tc, pmcd=pmcd):
                        e.wait_ge(d_hs, tc + 1)
                        e.activation(out=hsneg[0:1, :], in_=pmcd[0:1, 0:512],
                                     func=AF.Copy, scale=-ZP).then_inc(d_hse, 1)
                        for k in range(16):
                            hc, tt = k // 4, k % 4
                            e.wait_ge(d_ep1, 16 * tc + k + 1)
                            if k >= 2:
                                e.wait_ge(d_ep3, 16 * tc + k - 1)
                            e.activation(out=silb[k % 2][:], in_=ymb[k % 2][:],
                                         func=AF.Silu).then_inc(d_sil, 1)
                            e.wait_ge(d_ep3, 16 * tc + k + 1)
                            e.dma_start(
                                out=out[tc * 512 + tt * 128:
                                        tc * 512 + (tt + 1) * 128,
                                        hc * 512:(hc + 1) * 512],
                                in_=outst[k % 2][:]).then_inc(d_wr[k % 2], 16)
                        e.wait_ge(d_wr[0], 128 * (tc + 1))
                        e.wait_ge(d_wr[1], 128 * (tc + 1))

                    @blk.vector
                    def _(e, tc=tc, py=py):
                        if tc == 0:
                            e.wait_ge(c_dsc, 16)
                        for k in range(16):
                            hc, tt = k // 4, k % 4
                            e.wait_ge(d_mm, 16 * tc + k + 1)
                            if k >= 2:
                                e.wait_ge(d_sil, 16 * tc + k - 1)
                                e.wait_ge(d_ep3, 16 * tc + k - 1)
                            e.scalar_tensor_tensor(
                                out=ymb[k % 2][:], in0=py[tt][:], scalar=1.0,
                                in1=dscb[:, hc * 512:(hc + 1) * 512],
                                op0=ALU.mult, op1=ALU.mult).then_inc(d_ep1, 1)
                            e.wait_ge(d_sil, 16 * tc + k + 1)
                            e.wait_ge(d_xt[k % 2],
                                      128 * tc + 16 * (k // 2 + 1))
                            e.wait_ge(d_sx[k % 2],
                                      128 * tc + 16 * (k // 2 + 1))
                            if k >= 2:
                                e.wait_ge(d_wr[k % 2],
                                          128 * tc + 16 * (k // 2))
                            vmark(e.tensor_tensor(
                                out=ymb[k % 2][:], in0=silb[k % 2][:],
                                in1=stt_b[k % 2][:], op=ALU.mult))
                            vwait(e)
                            e.tensor_tensor(
                                out=outst[k % 2][:], in0=ymb[k % 2][:],
                                in1=xt[k % 2][:], op=ALU.add).then_inc(d_ep3, 1)
                dsec.close()
            cs.close()
    return nc


def _host_arrange(Wq, Wk, Wv, Wo, gate_w, up_w, down_w, gate_scale, up_scale,
                  down_scale):
    Wq = np.asarray(Wq, dtype=np.float32)
    Wk = np.asarray(Wk, dtype=np.float32)
    Wv = np.asarray(Wv, dtype=np.float32)
    Wo = np.asarray(Wo, dtype=np.float32)
    wqr = np.ascontiguousarray(
        Wq.reshape(NH, 128, NCI, 128).transpose(0, 3, 2, 1).reshape(NH, 128, D))
    wkr = np.ascontiguousarray(
        Wk.reshape(NH, 128, NCI, 128).transpose(0, 3, 2, 1).reshape(NH, 128, D))
    wvr = np.ascontiguousarray(
        Wv.reshape(4, 512, NCI, 128).transpose(0, 3, 2, 1).reshape(4, 128, FI))
    wor = np.ascontiguousarray(
        Wo.reshape(D, NH, 128).transpose(2, 1, 0))
    gwT = np.ascontiguousarray(np.asarray(gate_w, dtype=np.int32).T)
    uwT = np.ascontiguousarray(np.asarray(up_w, dtype=np.int32).T)
    dwT = np.ascontiguousarray(np.asarray(down_w, dtype=np.int32).T)
    gsc = np.ascontiguousarray(
        np.asarray(gate_scale, dtype=np.float32).reshape(64, 128))
    usc = np.ascontiguousarray(
        np.asarray(up_scale, dtype=np.float32).reshape(64, 128))
    dscr = np.ascontiguousarray(
        np.asarray(down_scale, dtype=np.float32).reshape(1, D))
    eye = np.eye(128, dtype=np.float32)
    return dict(wqr=wqr, wkr=wkr, wvr=wvr, wor=wor, gwT=gwT, uwT=uwT, dwT=dwT,
                gsc=gsc, usc=usc, dscr=dscr, eye=eye)


def kernel(L0_attn_hidden_states, L0_attn_Wq, L0_attn_Wk, L0_attn_Wv,
           L0_attn_Wo, L0_ffn_gate_w, L0_ffn_gate_scale, L0_ffn_up_w,
           L0_ffn_up_scale, L0_ffn_down_w, L0_ffn_down_scale):
    x = np.ascontiguousarray(np.asarray(L0_attn_hidden_states, dtype=np.float32))
    shared = _host_arrange(L0_attn_Wq, L0_attn_Wk, L0_attn_Wv, L0_attn_Wo,
                           L0_ffn_gate_w, L0_ffn_up_w, L0_ffn_down_w,
                           L0_ffn_gate_scale, L0_ffn_up_scale,
                           L0_ffn_down_scale)
    in_maps = []
    for c in range(8):
        b, h2 = c // 2, c % 2
        xb = x[b]
        if h2 == 0:
            xc = xb
        else:
            xc = np.ascontiguousarray(
                np.concatenate([xb[S_OWN:], xb[:S_OWN]], axis=0))
        m = dict(shared)
        m["x"] = xc
        in_maps.append(m)

    nc = build_program(dbg=DBG)
    res = run_bass_kernel_spmd(nc, in_maps, core_ids=list(range(8)))
    outp = np.empty((4, S_FULL, D), dtype=np.float32)
    for c in range(8):
        b, h2 = c // 2, c % 2
        outp[b, h2 * S_OWN:(h2 + 1) * S_OWN, :] = np.asarray(
            res.results[c]["out"], dtype=np.float32)
    return outp
